# version 03
# baseline (speedup 1.0000x reference)
import numpy as np
import ml_dtypes
from contextlib import ExitStack

import concourse.bass as bass
import concourse.mybir as mybir
from concourse.bass_utils import run_bass_kernel_spmd
from concourse.aot_env import allow_ici_all_to_all

F32 = mybir.dt.float32
BF16 = mybir.dt.bfloat16
I32 = mybir.dt.int32
AF = mybir.ActivationFunctionType
OP = mybir.AluOpType

B, S, DM = 2, 4096, 4096
T = B * S
TC = T // 8
H, G, D, R = 32, 8, 128, 4
QH = R * D
SCALE = 1.0 / float(np.sqrt(np.float64(128.0)))
EXPB = -20.0
NEG = -3.0e38
EPS = 1e-5
NT = TC // 128
NDM = DM // 128
NG = S // 512

DBG_BUILD = [False]


class Px:
    def __init__(self):
        self.eng = None

    def __getattr__(self, name):
        if self.eng is not None:
            return getattr(self.eng, name)
        return lambda *a, **kw: None


class Sem:
    def __init__(self, h):
        self.h = h
        self.n = 0

    def inc(self, inst, k=1):
        if inst is not None:
            inst.then_inc(self.h, k)
        self.n += k
        return self.n

    def wait(self, px, val=None):
        if px.eng is not None:
            px.eng.wait_ge(self.h, self.n if val is None else val)


def build_nc(dbg=False):
    nc = bass.Bass("TRN2", target_bir_lowering=False, debug=False)

    xck = nc.dram_tensor("xck", [TC, DM], F32, kind="ExternalInput").ap()
    wqd = nc.dram_tensor("wqd", [DM, QH], F32, kind="ExternalInput").ap()
    bqv = nc.dram_tensor("bqv", [128, R], F32, kind="ExternalInput").ap()
    wkd = nc.dram_tensor("wkd", [DM, D], F32, kind="ExternalInput").ap()
    bkv = nc.dram_tensor("bkv", [128, 1], F32, kind="ExternalInput").ap()
    wvd = nc.dram_tensor("wvd", [DM, D], F32, kind="ExternalInput").ap()
    bvv = nc.dram_tensor("bvv", [128, 1], F32, kind="ExternalInput").ap()
    wod = nc.dram_tensor("wod", [DM, DM], F32, kind="ExternalInput").ap()
    bod = nc.dram_tensor("bod", [1, DM], F32, kind="ExternalInput").ap()
    pkd = nc.dram_tensor("pkd", [512, 2048], I32, kind="ExternalInput").ap()
    scd = nc.dram_tensor("scd", [512, 32], F32, kind="ExternalInput").ap()
    fbd = nc.dram_tensor("fbd", [1, DM], F32, kind="ExternalInput").ap()
    cosd = nc.dram_tensor("cosd", [64, S], F32, kind="ExternalInput").ap()
    sind = nc.dram_tensor("sind", [64, S], F32, kind="ExternalInput").ap()
    cmd = nc.dram_tensor("cmd", [128, 4 * 512], F32, kind="ExternalInput").ap()
    out = nc.dram_tensor("out", [TC, DM], F32, kind="ExternalOutput").ap()

    agx_in = nc.dram_tensor("agx_in", [NDM * NT, 128, 128], BF16)
    agx_out = nc.dram_tensor("agx_out", [8 * NDM * NT, 128, 128], BF16)
    agw_in = nc.dram_tensor("agw_in", [DM, 512], BF16)
    agw_out = nc.dram_tensor("agw_out", [8 * DM, 512], BF16)
    a2a_in = nc.dram_tensor("a2a_in", [8, QH, TC], BF16)
    a2a_out = nc.dram_tensor("a2a_out", [8, QH, TC], BF16)
    z1scr = nc.dram_tensor("z1scr", [TC, DM], F32)
    x1scr = nc.dram_tensor("x1scr", [TC, DM], F32)
    z2scr = nc.dram_tensor("z2scr", [TC, DM], F32)

    dbg_t = {}
    if dbg:
        dbg_t["qT"] = nc.dram_tensor("d_qT", [128, R * S], F32, kind="ExternalOutput").ap()
        dbg_t["kT"] = nc.dram_tensor("d_kT", [128, T], F32, kind="ExternalOutput").ap()
        dbg_t["vb"] = nc.dram_tensor("d_vb", [128, T], F32, kind="ExternalOutput").ap()
        dbg_t["ctx"] = nc.dram_tensor("d_ctx", [8 * QH, TC], F32, kind="ExternalOutput").ap()
        dbg_t["z1"] = nc.dram_tensor("d_z1", [TC, DM], F32, kind="ExternalOutput").ap()
        dbg_t["x1"] = nc.dram_tensor("d_x1", [TC, DM], F32, kind="ExternalOutput").ap()
        dbg_t["wt"] = nc.dram_tensor("d_wt", [DM, 512], F32, kind="ExternalOutput").ap()
        dbg_t["win"] = nc.dram_tensor("d_win", [DM, 512], F32, kind="ExternalOutput").ap()
        dbg_t["wtile0"] = nc.dram_tensor("d_wtile0", [128, DM], F32, kind="ExternalOutput").ap()
        dbg_t["wtile1"] = nc.dram_tensor("d_wtile1", [128, DM], F32, kind="ExternalOutput").ap()

    st = ExitStack()

    def sball(stk, nm, sh, dt):
        return stk.enter_context(nc.sbuf_tensor(nm, sh, dt))

    sY = Sem(st.enter_context(nc.semaphore("sY")))
    sG = Sem(st.enter_context(nc.semaphore("sG")))
    sD = Sem(st.enter_context(nc.semaphore("sD")))
    sT = Sem(st.enter_context(nc.semaphore("sT")))
    sV = Sem(st.enter_context(nc.semaphore("sV")))
    sS = Sem(st.enter_context(nc.semaphore("sS")))
    sC = Sem(st.enter_context(nc.semaphore("sC")))
    sO = Sem(st.enter_context(nc.semaphore("sO")))
    s_init = Sem(st.enter_context(nc.semaphore("s_init")))
    s_bias = Sem(st.enter_context(nc.semaphore("s_bias")))
    s_cm = Sem(st.enter_context(nc.semaphore("s_cm")))
    s_tab = Sem(st.enter_context(nc.semaphore("s_tab")))
    s_ct = Sem(st.enter_context(nc.semaphore("s_ct")))
    s_st = [Sem(st.enter_context(nc.semaphore("s_st%d" % i))) for i in range(4)]
    s_xf = [Sem(st.enter_context(nc.semaphore("s_xf%d" % i))) for i in range(2)]
    s_xg = [Sem(st.enter_context(nc.semaphore("s_xg%d" % i))) for i in range(2)]
    s_qd = [Sem(st.enter_context(nc.semaphore("s_qd%d" % i))) for i in range(2)]
    s_o2 = [Sem(st.enter_context(nc.semaphore("s_o2%d" % i))) for i in range(2)]
    s_wo = [Sem(st.enter_context(nc.semaphore("s_wo%d" % i))) for i in range(2)]
    s_xr = [Sem(st.enter_context(nc.semaphore("s_xr%d" % i))) for i in range(2)]
    s_zs = [Sem(st.enter_context(nc.semaphore("s_zs%d" % i))) for i in range(2)]
    all_sems = ([sY, sG, sD, sT, sV, sS, sC, sO, s_init, s_bias, s_cm, s_tab, s_ct]
                + s_st + s_xf + s_xg + s_qd + s_o2 + s_wo + s_xr + s_zs)

    pbank = [st.enter_context(nc.psum_tensor("pb%d" % i, [128, 512], F32))
             for i in range(8)]

    identf = sball(st, "identf", [128, 128], F32)
    io_a = sball(st, "io_a", [128, 128], I32)
    io_b = sball(st, "io_b", [128, 128], I32)
    onecol = sball(st, "onecol", [128, 1], BF16)
    onerow = sball(st, "onerow", [1, 128], F32)
    cexpb = sball(st, "cexpb", [128, 1], F32)
    cepsb = sball(st, "cepsb", [128, 1], F32)
    bq_sb = sball(st, "bq_sb", [128, R], F32)
    bk_sb = sball(st, "bk_sb", [128, 1], F32)
    bv_sb = sball(st, "bv_sb", [128, 1], F32)

    def run_phase(fn):
        snap = [s_.n for s_ in all_sems]

        def mk(kind):
            def f(eng):
                for s_, v in zip(all_sems, snap):
                    s_.n = v
                fn(kind, eng)
            return f

        blk = nc.Block()
        blk.__enter__()
        blk.gpsimd(mk('g'))
        blk.vector(mk('v'))
        blk.scalar(mk('s'))
        blk.tensor(mk('t'))
        blk.sync(mk('y'))
        blk.__exit__(None, None, None)

    marks = {}

    ph0 = ExitStack()
    xf32 = [sball(ph0, "xf%d" % i, [128, DM], F32) for i in range(2)]
    xT128 = [sball(ph0, "xT%d" % i, [128, 128], BF16) for i in range(4)]
    pkt = sball(ph0, "pkt", [128, 2048], I32)
    sct = sball(ph0, "sct", [128, 32], F32)
    d_n = sball(ph0, "d_n", [128, 1024], I32)
    d_s = sball(ph0, "d_s", [128, 1024], I32)
    d_i = sball(ph0, "d_i", [128, 1024], I32)
    d_e = sball(ph0, "d_e", [128, 1024], I32)
    d_m = sball(ph0, "d_m", [128, 1024], I32)
    d_a = sball(ph0, "d_a", [128, 1024], F32)
    d_u = sball(ph0, "d_u", [128, 1024], F32)
    d_q = sball(ph0, "d_q", [128, 1024], F32)
    d_t = sball(ph0, "d_t", [128, 1024], F32)
    d_g = sball(ph0, "d_g", [128, 1024], F32)
    wtile = sball(ph0, "wtile", [128, DM], F32)
    wT128 = [sball(ph0, "wT%d" % i, [128, 128], BF16) for i in range(4)]

    def phase0(kind, eng):
        PXs = {k: Px() for k in "gvsty"}
        PXs[kind].eng = eng
        GE, V, SE, TE, Y = PXs['g'], PXs['v'], PXs['s'], PXs['t'], PXs['y']

        GE.memset(onecol[:], 1.0)
        GE.memset(onerow[:], 1.0)
        GE.memset(cexpb[:], EXPB)
        GE.memset(cepsb[:], EPS)
        GE.iota(io_a[:], [[1, 128]], base=0, channel_multiplier=0,
                allow_small_or_imprecise_dtypes=True)
        s_init.inc(GE.iota(io_b[:], [[0, 128]], base=0, channel_multiplier=1,
                           allow_small_or_imprecise_dtypes=True))
        s_bias.inc(GE.dma_start(out=bq_sb[:], in_=bqv), 16)
        s_bias.inc(GE.dma_start(out=bk_sb[:], in_=bkv), 16)
        s_bias.inc(GE.dma_start(out=bv_sb[:], in_=bvv), 16)
        s_init.wait(V)
        idm = sV.inc(V.tensor_tensor(identf[:], io_a[:], io_b[:], OP.is_equal))
        sV.wait(TE, idm)

        trm = {}
        evm = {}
        stm = {}
        for t in range(NT):
            if t >= 2:
                sT.wait(Y, trm[(t - 2) * NDM + NDM - 1])
            ldx = s_xf[t % 2].inc(Y.dma_start(out=xf32[t % 2][:],
                                              in_=xck[t * 128:(t + 1) * 128, :]), 16)
            for db in range(NDM):
                i = t * NDM + db
                s_xf[t % 2].wait(TE, ldx)
                if i >= 4:
                    sV.wait(TE, evm[i - 4])
                trm[i] = sT.inc(TE.transpose(
                    pbank[4 + i % 4][:, 0:128],
                    xf32[t % 2][:, db * 128:(db + 1) * 128], identf[:]))
                sT.wait(V, trm[i])
                if i >= 4:
                    s_st[i % 4].wait(V, stm[i - 4])
                evm[i] = sV.inc(V.tensor_copy(xT128[i % 4][:], pbank[4 + i % 4][:, 0:128]))
                sV.wait(Y, evm[i])
                stm[i] = s_st[i % 4].inc(Y.dma_start(
                    out=agx_in.ap()[db * NT + t], in_=xT128[i % 4][:]), 16)
        for s_ in s_st:
            s_.wait(GE)
        marks["agx"] = sC.inc(GE.collective_compute(
            "AllGather", OP.bypass, replica_groups=[list(range(8))],
            ins=[agx_in.ap().opt()], outs=[agx_out.ap().opt()]))

        wtrm = {}
        wevm = {}
        wstm = {}
        vdeq = {}
        for ot in range(4):
            if ot >= 1:
                sV.wait(GE, vdeq[ot - 1])
            lp = sG.inc(GE.dma_start(out=pkt[:], in_=pkd[ot * 128:(ot + 1) * 128, :]), 16)
            lp = sG.inc(GE.dma_start(out=sct[:], in_=scd[ot * 128:(ot + 1) * 128, :]), 16)
            sG.wait(V, lp)
            if ot >= 1:
                sT.wait(V, wtrm[(ot - 1) * NDM + NDM - 1])
            for half in range(2):
                co = half * 1024
                pslice = pkt[:, co:co + 1024]
                for hi in range(2):
                    if hi == 0:
                        V.tensor_single_scalar(d_n[:], pslice, 4, OP.logical_shift_right)
                    else:
                        V.tensor_single_scalar(d_n[:], pslice, 15, OP.bitwise_and)
                    V.tensor_single_scalar(d_s[:], d_n[:], 3, OP.logical_shift_right)
                    V.tensor_single_scalar(d_i[:], d_n[:], 7, OP.bitwise_and)
                    V.tensor_single_scalar(d_e[:], d_i[:], 1, OP.logical_shift_right)
                    V.tensor_single_scalar(d_m[:], d_i[:], 1, OP.bitwise_and)
                    V.tensor_scalar(d_a[:], d_e[:], 2.0, 2.0, OP.mult, OP.min)
                    V.tensor_scalar(d_u[:], d_e[:], 0.5, 0.5, OP.mult, OP.max)
                    V.tensor_scalar_add(d_u[:], d_u[:], -0.25)
                    V.tensor_mul(d_q[:], d_u[:], d_u[:])
                    V.scalar_tensor_tensor(d_t[:], d_m[:], 1.0, d_a[:], OP.mult, OP.add)
                    V.scalar_tensor_tensor(d_g[:], d_q[:], 0.4375, d_t[:], OP.add, OP.mult)
                    V.tensor_scalar(d_t[:], d_s[:], -2.0, 1.0, OP.mult, OP.add)
                    V.tensor_mul(d_g[:], d_g[:], d_t[:])
                    ssrc = bass.AP(sct, half * 16, [[32, 128], [1, 16], [0, 64]])
                    dsts = bass.AP(wtile, 2 * co + hi, [[DM, 128], [2, 1024]])
                    V.tensor_tensor(dsts, d_g[:], ssrc, OP.mult)
            vdeq[ot] = sV.inc(V.engine_nop())
            if DBG_BUILD[0] and ot == 0:
                sV.wait(GE, vdeq[0])
                sO.inc(GE.dma_start(out=dbg_t["wtile0"], in_=wtile[:]), 16)
            for ib in range(NDM):
                i = ot * NDM + ib
                sV.wait(TE, vdeq[ot])
                if i >= 4:
                    sV.wait(TE, wevm[i - 4])
                wtrm[i] = sT.inc(TE.transpose(
                    pbank[4 + i % 4][:, 0:128],
                    wtile[:, ib * 128:(ib + 1) * 128], identf[:]))
                sT.wait(V, wtrm[i])
                if i >= 4:
                    s_st[i % 4].wait(V, wstm[i - 4])
                wevm[i] = sV.inc(V.tensor_copy(wT128[i % 4][:], pbank[4 + i % 4][:, 0:128]))
                sV.wait(Y, wevm[i])
                wstm[i] = s_st[i % 4].inc(Y.dma_start(
                    out=agw_in.ap()[ib * 128:(ib + 1) * 128, ot * 128:(ot + 1) * 128],
                    in_=wT128[i % 4][:]), 16)
        if DBG_BUILD[0]:
            sT.wait(GE, wtrm[NDM - 1])
            sO.inc(GE.dma_start(out=dbg_t["wtile1"], in_=wtile[:]), 16)
        for s_ in s_st:
            s_.wait(GE)
        sC.wait(GE, marks["agx"])
        marks["agw"] = sC.inc(GE.collective_compute(
            "AllGather", OP.bypass, replica_groups=[list(range(8))],
            ins=[agw_in.ap().opt()], outs=[agw_out.ap().opt()]))

    run_phase(phase0)
    ph0.close()

    stA = ExitStack()
    wq_sb = sball(stA, "wq_sb", [128, NDM * QH], BF16)
    wk_sb = sball(stA, "wk_sb", [128, NDM * D], BF16)
    wv_sb = sball(stA, "wv_sb", [128, NDM * D], BF16)
    csl = [sball(stA, "csl0", [64, 512], F32)]
    snl = [sball(stA, "snl0", [64, 512], F32)]
    cm_sb = sball(stA, "cm_sb", [128, 4 * 512], BF16)
    qT_sb = sball(stA, "qT_sb", [128, R * S], BF16)
    kT_sb = sball(stA, "kT_sb", [128, T], BF16)
    vb_sb = sball(stA, "vb_sb", [128, T], BF16)
    xg = [sball(stA, "xg%d" % i, [128, 16, 512], BF16) for i in range(2)]
    qa = [sball(stA, "qa%d" % i, [128, 512], F32) for i in range(2)]
    qb = [sball(stA, "qb%d" % i, [64, 512], F32) for i in range(2)]
    rp = [sball(stA, "rp%d" % i, [64, 512], F32) for i in range(2)]
    o2s = [sball(stA, "o2s%d" % i, [64, 512], BF16) for i in range(2)]
    vst = sball(stA, "vst", [128, 512], F32)
    prT = sball(stA, "prT", [128, 32 * 512], BF16)
    dstg = [sball(stA, "dstg0", [128, 512], F32)]
    ctxe = [sball(stA, "ctxe0", [128, 512], BF16)]

    def phase1(b):
        def fn(kind, eng):
            PXs = {k: Px() for k in "gvsty"}
            PXs[kind].eng = eng
            GE, V, SE, TE, Y = PXs['g'], PXs['v'], PXs['s'], PXs['t'], PXs['y']

            if b == 0:
                for dmc in range(NDM):
                    sG.inc(GE.dma_start(out=wq_sb[:, dmc * QH:(dmc + 1) * QH],
                                        in_=wqd[dmc * 128:(dmc + 1) * 128, :]), 16)
                    sG.inc(GE.dma_start(out=wk_sb[:, dmc * D:(dmc + 1) * D],
                                        in_=wkd[dmc * 128:(dmc + 1) * 128, :]), 16)
                    sG.inc(GE.dma_start(out=wv_sb[:, dmc * D:(dmc + 1) * D],
                                        in_=wvd[dmc * 128:(dmc + 1) * 128, :]), 16)
                s_cm.inc(GE.dma_start(out=cm_sb[:], in_=cmd), 16)
                marks["wqkv"] = sG.n
                marks["cmld"] = s_cm.n

            mmhalf = {}
            ldm = {}
            evfb = {}
            ropev = {}
            roped = {}
            qadma = {}
            vtr = {}
            vev = {}
            for tg in range(8):
                gtok = b * S + tg * 512
                rank = gtok // TC
                coff = gtok % TC
                cl = tg * 512
                if tg >= 1:
                    sV.wait(Y, ropev[(tg - 1) * 5 + 4])
                s_tab.inc(Y.dma_start(out=csl[0][:], in_=cosd[:, cl:cl + 512]), 16)
                tabm = s_tab.inc(Y.dma_start(out=snl[0][:], in_=sind[:, cl:cl + 512]), 16)
                for half in range(2):
                    if tg > 0:
                        sT.wait(Y, mmhalf[(tg - 1, half)])
                    lm = 0
                    sC.wait(Y, marks["agx"])
                    for j in range(16):
                        dmc = half * 16 + j
                        for k4 in range(4):
                            lm = s_xg[half].inc(Y.dma_start(
                                out=xg[half][:, j, k4 * 128:(k4 + 1) * 128],
                                in_=agx_out.ap()[rank * NDM * NT + dmc * NT
                                                 + coff // 128 + k4]), 16)
                    ldm[(tg, half)] = lm
                sG.wait(TE, marks["wqkv"])
                for half in range(2):
                    s_xg[half].wait(TE, ldm[(tg, half)])
                    for j in range(16):
                        dmc = half * 16 + j
                        first = (dmc == 0)
                        last = (dmc == NDM - 1)
                        if first and tg > 0:
                            for bkk in range(6):
                                sV.wait(TE, evfb[(tg - 1, bkk)])
                        TE.matmul(pbank[0][:, 0:512],
                                  wq_sb[:, dmc * QH + 0 * 128:dmc * QH + 1 * 128],
                                  xg[half][:, j, :], start=first, stop=last)
                        TE.matmul(pbank[1][:, 0:512],
                                  wq_sb[:, dmc * QH + 1 * 128:dmc * QH + 2 * 128],
                                  xg[half][:, j, :], start=first, stop=last)
                        TE.matmul(pbank[2][:, 0:512],
                                  wq_sb[:, dmc * QH + 2 * 128:dmc * QH + 3 * 128],
                                  xg[half][:, j, :], start=first, stop=last)
                        TE.matmul(pbank[3][:, 0:512],
                                  wq_sb[:, dmc * QH + 3 * 128:dmc * QH + 4 * 128],
                                  xg[half][:, j, :], start=first, stop=last)
                        TE.matmul(pbank[4][:, 0:512],
                                  wk_sb[:, dmc * D:(dmc + 1) * D],
                                  xg[half][:, j, :], start=first, stop=last)
                        inst = TE.matmul(pbank[5][:, 0:512],
                                         wv_sb[:, dmc * D:(dmc + 1) * D],
                                         xg[half][:, j, :], start=first, stop=last)
                        if j == 15:
                            mmhalf[(tg, half)] = sT.inc(inst)
                mmend = mmhalf[(tg, 1)]
                sT.wait(V, mmend)
                if b == 0 and tg == 0:
                    s_bias.wait(V, 48)
                s_tab.wait(V, tabm)
                cs = csl[0][:]
                sn = snl[0][:]
                for fb in range(5):
                    qi = fb % 2
                    tix = tg * 5 + fb
                    bias = bq_sb[:, fb:fb + 1] if fb < 4 else bk_sb[:, 0:1]
                    prev_qi = tix - {0: 1, 1: 3, 2: 2, 3: 2, 4: 2}[fb]
                    if prev_qi >= 0:
                        s_qd[qi].wait(V, qadma.get(prev_qi, 0) or 0)
                    evfb[(tg, fb)] = sV.inc(V.tensor_scalar(
                        qa[qi][:], pbank[fb][:, 0:512], bias, None, OP.add))
                    sV.wait(SE, evfb[(tg, fb)])
                    if prev_qi >= 0:
                        sV.wait(SE, ropev.get(prev_qi, 0) or 0)
                    qadma[tix] = s_qd[qi].inc(SE.dma_start(out=qb[qi][:], in_=qa[qi][64:128, :]), 16)
                    s_qd[qi].wait(V, qadma[tix])
                    dst_lo = (qT_sb[0:64, fb * S + cl:fb * S + cl + 512] if fb < 4
                              else kT_sb[0:64, gtok:gtok + 512])
                    V.tensor_tensor(rp[0][:], qa[qi][0:64, :], cs, OP.mult)
                    V.tensor_tensor(rp[1][:], qb[qi][:], sn, OP.mult)
                    V.tensor_sub(dst_lo, rp[0][:], rp[1][:])
                    V.tensor_tensor(rp[0][:], qa[qi][0:64, :], sn, OP.mult)
                    V.tensor_tensor(rp[1][:], qb[qi][:], cs, OP.mult)
                    if tix >= 2:
                        s_o2[tix % 2].wait(V, roped.get(tix - 2, 0) or 0)
                    ropev[tix] = sV.inc(V.tensor_add(o2s[tix % 2][:], rp[0][:], rp[1][:]))
                    sV.wait(SE, ropev[tix])
                    if fb < 4:
                        dst_hi = qT_sb[64:128, fb * S + cl:fb * S + cl + 512]
                    else:
                        dst_hi = kT_sb[64:128, gtok:gtok + 512]
                    roped[tix] = s_o2[tix % 2].inc(SE.dma_start(out=dst_hi, in_=o2s[tix % 2][:]), 16)
                if tg > 0:
                    sT.wait(V, vtr[(tg - 1) * 4 + 3])
                evfb[(tg, 5)] = sV.inc(V.tensor_scalar(
                    vst[:], pbank[5][:, 0:512], bv_sb[:, 0:1], None, OP.add))
                for j in range(4):
                    i = tg * 4 + j
                    sV.wait(TE, evfb[(tg, 5)])
                    if i >= 2:
                        sV.wait(TE, vev[i - 2])
                    vtr[i] = sT.inc(TE.transpose(pbank[6 + i % 2][:, 0:128],
                                                 vst[:, j * 128:(j + 1) * 128], identf[:]))
                    sT.wait(V, vtr[i])
                    vev[i] = sV.inc(V.tensor_copy(
                        vb_sb[:, gtok + j * 128:gtok + (j + 1) * 128],
                        pbank[6 + i % 2][:, 0:128]))
            marks["ropeD%d" % b] = (s_o2[0].n, s_o2[1].n, s_qd[0].n, s_qd[1].n)
        return fn

    def phase2(b):
        def fn(kind, eng):
            PXs = {k: Px() for k in "gvsty"}
            PXs[kind].eng = eng
            GE, V, SE, TE, Y = PXs['g'], PXs['v'], PXs['s'], PXs['t'], PXs['y']
            boff = b * S
            m0, m1, m2, m3 = marks["ropeD%d" % b]
            s_o2[0].wait(TE, m0)
            s_o2[1].wait(TE, m1)
            s_qd[0].wait(TE, m2)
            s_qd[1].wait(TE, m3)
            s_cm.wait(V, marks["cmld"])

            psfree = {0: None, 1: None}
            prread = {}
            dnfree = None
            bcfree = None
            cxfree = None
            ctfree = {0: None}
            dgfree = {0: None}

            for grp in range(NG):
                nkb = 4 * (grp + 1)
                gq = boff + grp * 512
                dest = gq // TC
                qoff = gq % TC
                for h in range(4):
                    expm = {}
                    dn_in = None
                    pv_in = None
                    for kb in range(nkb + 2):
                        if kb < nkb:
                            slot = kb % 2
                            if psfree[slot] is not None:
                                w, val = psfree[slot]
                                (sS if w == 's' else sV).wait(TE, val)
                            mmm = sT.inc(TE.matmul(
                                pbank[slot][:, 0:512],
                                kT_sb[:, boff + kb * 128:boff + (kb + 1) * 128],
                                qT_sb[:, h * S + grp * 512:h * S + (grp + 1) * 512],
                                start=True, stop=True))
                            diag = kb >= nkb - 4
                            if not diag:
                                sT.wait(SE, mmm)
                                if prread.get(kb) is not None:
                                    sT.wait(SE, prread[kb])
                                expm[kb] = sS.inc(SE.activation(
                                    prT[:, kb * 512:(kb + 1) * 512], pbank[slot][:, 0:512],
                                    AF.Exp, bias=cexpb[:], scale=SCALE))
                                psfree[slot] = ('s', expm[kb])
                            else:
                                oi = kb - (nkb - 4)
                                sT.wait(V, mmm)
                                if dgfree[0] is not None:
                                    sS.wait(V, dgfree[0])
                                stm_ = sV.inc(V.scalar_tensor_tensor(
                                    dstg[0][:], pbank[slot][:, 0:512], SCALE,
                                    cm_sb[:, oi * 512:(oi + 1) * 512], OP.mult, OP.add))
                                psfree[slot] = ('v', stm_)
                                sV.wait(SE, stm_)
                                if prread.get(kb) is not None:
                                    sT.wait(SE, prread[kb])
                                expm[kb] = sS.inc(SE.activation(
                                    prT[:, kb * 512:(kb + 1) * 512], dstg[0][:],
                                    AF.Exp, bias=cexpb[:], scale=1.0))
                                dgfree[0] = expm[kb]
                        kd = kb - 2
                        if 0 <= kd < nkb:
                            sS.wait(TE, expm[kd])
                            if kd == 0:
                                if dnfree is not None:
                                    sV.wait(TE, dnfree)
                                if cxfree is not None:
                                    sV.wait(TE, cxfree)
                            dn_in = TE.matmul(pbank[2][0:1, 0:512], onecol[:],
                                              prT[:, kd * 512:(kd + 1) * 512],
                                              start=(kd == 0), stop=(kd == nkb - 1))
                            pv_in = TE.matmul(pbank[4][:, 0:512],
                                              vb_sb[:, boff + kd * 128:boff + (kd + 1) * 128],
                                              prT[:, kd * 512:(kd + 1) * 512],
                                              start=(kd == 0), stop=(kd == nkb - 1))
                            prread[kd] = sT.inc(pv_in)
                    em = sT.n
                    sT.wait(V, em)
                    V.tensor_copy(rp[0][0:1, :], pbank[2][0:1, 0:512])
                    dn2 = sV.inc(V.reciprocal(rp[1][0:1, :], rp[0][0:1, :]))
                    dnfree = dn2
                    if bcfree is not None:
                        sV.wait(TE, bcfree)
                    sV.wait(TE, dn2)
                    dbm = sT.inc(TE.matmul(pbank[3][:, 0:512], onerow[:],
                                           rp[1][0:1, :], start=True, stop=True))
                    ci = 0
                    sT.wait(V, dbm)
                    bcm = sV.inc(V.tensor_copy(vst[:, 0:512], pbank[3][:, 0:512]))
                    bcfree = bcm
                    if ctfree[ci] is not None:
                        sY.wait(V, ctfree[ci])
                    fm = sV.inc(V.tensor_tensor(ctxe[ci][:], pbank[4][:, 0:512],
                                                vst[:, 0:512], OP.mult))
                    cxfree = fm
                    sV.wait(Y, fm)
                    ctfree[ci] = sY.inc(Y.dma_start(
                        out=a2a_in.ap()[dest, h * 128:(h + 1) * 128, qoff:qoff + 512],
                        in_=ctxe[ci][:]), 16)
        return fn

    run_phase(phase1(0))
    run_phase(phase2(0))
    run_phase(phase1(1))
    run_phase(phase2(1))

    def phase_a2a(kind, eng):
        PXs = {k: Px() for k in "gvsty"}
        PXs[kind].eng = eng
        GE = PXs['g']
        sY.wait(GE)
        with allow_ici_all_to_all():
            marks["a2a"] = sC.inc(GE.collective_compute(
                "AllToAll", OP.bypass, replica_groups=[list(range(8))],
                ins=[a2a_in.ap().opt()], outs=[a2a_out.ap().opt()]))
        if DBG_BUILD[0]:
            GE.wait_ge(sC.h, marks["a2a"])
            sO.inc(GE.dma_start(out=dbg_t["ctx"],
                                in_=a2a_out.ap().rearrange("a b c -> (a b) c")), 16)
            sO.inc(GE.dma_start(out=dbg_t["qT"], in_=qT_sb[:]), 16)
            sO.inc(GE.dma_start(out=dbg_t["kT"], in_=kT_sb[:]), 16)
            sO.inc(GE.dma_start(out=dbg_t["vb"], in_=vb_sb[:]), 16)
            sO.inc(GE.dma_start(out=dbg_t["wt"], in_=agw_out.ap()[0:DM, :]), 16)
            sO.inc(GE.dma_start(out=dbg_t["win"], in_=agw_in.ap()), 16)
            sO.wait(GE)

    run_phase(phase_a2a)
    stA.close()

    stW = ExitStack()
    woh = sball(stW, "woh", [128, NDM, 512], BF16)
    bob_sb = sball(stW, "bob_sb", [128, DM], F32)
    fbb_sb = sball(stW, "fbb_sb", [128, DM], F32)
    zc = [sball(stW, "zc%d" % i, [128, 512], F32) for i in range(2)]
    zd = [sball(stW, "zd%d" % i, [128, 512], F32) for i in range(2)]
    sqd = sball(stW, "sqd", [128, 512], F32)
    xres = [sball(stW, "xres%d" % i, [128, 512], F32) for i in range(2)]
    sums = sball(stW, "sums", [128, NT * 8], F32)
    sqs = sball(stW, "sqs", [128, NT * 8], F32)
    stat = [sball(stW, "stat%d" % i, [128, 128], F32) for i in range(6)]
    r1c = sball(stW, "r1c", [128, 1], F32)
    r2c = sball(stW, "r2c", [128, 1], F32)
    zfull = sball(stW, "zfull", [128, DM], F32)
    x1t = sball(stW, "x1t", [128, DM], F32)

    stC = ExitStack()
    ctxT = sball(stC, "ctxT", [128, NDM * TC], BF16)
    bf_row = sball(stC, "bf_row", [1, DM], F32)

    def mm_part(kind, eng, *, rhs_src, lhs_sb, bias_sb, res_src, zdst,
                lhs_wait, agw_wait, markpfx):
        PXs = {k: Px() for k in "gvsty"}
        PXs[kind].eng = eng
        GE, V, SE, TE, Y = PXs['g'], PXs['v'], PXs['s'], PXs['t'], PXs['y']

        if markpfx == "p3":
            sC.wait(Y, marks["a2a"])
            lm = 0
            a2f = a2a_out.ap().rearrange("a b c -> (a b) c")
            for fc in range(NDM):
                lm = s_ct.inc(Y.dma_start(out=ctxT[:, fc * TC:(fc + 1) * TC],
                                          in_=a2f[fc * 128:(fc + 1) * 128, :]), 16)
            marks["ctxT"] = lm
            for (src_row, dstsb) in ((bod, bob_sb), (fbd, fbb_sb)):
                bfl = s_bias.inc(GE.dma_start(out=bf_row[:], in_=src_row), 16)
                s_bias.wait(TE, bfl)
                mlast = None
                for w in range(8):
                    m1 = sT.inc(TE.matmul(pbank[2][:, 0:512], onerow[:],
                                          bf_row[0:1, w * 512:(w + 1) * 512],
                                          start=True, stop=True))
                    sT.wait(V, m1)
                    m2 = sV.inc(V.tensor_copy(dstsb[:, w * 512:(w + 1) * 512],
                                              pbank[2][:, 0:512]))
                    sV.wait(TE, m2)
                    mlast = m1
                sT.wait(GE, mlast)
            marks["bcb"] = sV.n
        else:
            sO.wait(Y, marks["x1sO"])

        wldm = {}
        mmend = {}
        e2m = {}
        sqm = {}
        stm_ = {}
        xldm = {}
        for dw in range(8):
            for half in range(2):
                if dw > 0:
                    sT.wait(GE, mmend[(dw - 1, half)])
                if agw_wait:
                    sC.wait(GE, marks["agw"])
                lm = 0
                for fc in range(half * 16, half * 16 + 16):
                    lm = s_wo[half].inc(GE.dma_start(out=woh[:, fc, :], in_=rhs_src(dw, fc)), 16)
                wldm[(dw, half)] = lm
            for tt in range(NT):
                bk = tt % 2
                i34 = dw * NT + tt
                if i34 >= 2:
                    sV.wait(Y, e2m[i34 - 2])
                xldm[i34] = s_xr[bk].inc(Y.dma_start(out=xres[bk][:], in_=res_src(dw, tt)), 16)
                if lhs_wait == "ctxT":
                    s_ct.wait(TE, marks["ctxT"])
                if i34 >= 2:
                    sV.wait(TE, e2m[i34 - 2])
                for half in range(2):
                    s_wo[half].wait(TE, wldm[(dw, half)])
                    inst = None
                    for fc in range(half * 16, half * 16 + 16):
                        inst = TE.matmul(pbank[bk][:, 0:512],
                                         lhs_sb[:, fc * TC + tt * 128:fc * TC + (tt + 1) * 128],
                                         woh[:, fc, :],
                                         start=(fc == 0), stop=(fc == NDM - 1))
                    mmend[(dw, half)] = sT.inc(inst)
                m = mmend[(dw, 1)]
                sT.wait(V, m)
                if markpfx == "p3":
                    sV.wait(V, marks["bcb"])
                s_xr[bk].wait(V, xldm[i34])
                if i34 >= 2:
                    sS.wait(V, sqm[i34 - 2])
                    s_zs[bk].wait(V, stm_[i34 - 2])
                V.tensor_tensor(zc[bk][:], pbank[bk][:, 0:512],
                                bias_sb[:, dw * 512:(dw + 1) * 512], OP.add)
                e2m[i34] = sV.inc(V.scalar_tensor_tensor(
                    zd[bk][:], zc[bk][:], 1.0, xres[bk][:], OP.mult, OP.add,
                    accum_out=sums[:, tt * 8 + dw:tt * 8 + dw + 1]))
                sV.wait(SE, e2m[i34])
                sqm[i34] = sS.inc(SE.activation(
                    sqd[:], zd[bk][:], AF.Square,
                    accum_out=sqs[:, tt * 8 + dw:tt * 8 + dw + 1]))
                sS.wait(Y, sqm[i34])
                stm_[i34] = s_zs[bk].inc(Y.dma_start(out=zdst(dw, tt), in_=zd[bk][:]), 16)
        marks[markpfx + "sS"] = sS.n
        marks[markpfx + "sO"] = (s_zs[0].n, s_zs[1].n)

    def ln_part(kind, eng, *, zsrc, out_dst, dbg_pair, x1T_build, markpfx):
        PXs = {k: Px() for k in "gvsty"}
        PXs[kind].eng = eng
        GE, V, SE, TE, Y = PXs['g'], PXs['v'], PXs['s'], PXs['t'], PXs['y']
        xm = {}
        x1trm = {}
        x1evm = {}
        x1st = {}
        tr_i = 0
        z0, z1_ = marks[markpfx + "sO"]
        s_zs[0].wait(Y, z0)
        s_zs[1].wait(Y, z1_)
        for tt in range(NT):
            s_zs[0].wait(V, z0)
            s_zs[1].wait(V, z1_)
            sS.wait(V, marks[markpfx + "sS"])
            V.tensor_reduce(r1c[:], sums[:, tt * 8:tt * 8 + 8], mybir.AxisListType.X, OP.add)
            V.tensor_reduce(r2c[:], sqs[:, tt * 8:tt * 8 + 8], mybir.AxisListType.X, OP.add)
            V.tensor_copy(stat[0][:], bass.AP(r1c, 0, [[1, 128], [0, 128]]))
            V.tensor_copy(stat[1][:], bass.AP(r2c, 0, [[1, 128], [0, 128]]))
            V.tensor_scalar(stat[2][:], stat[0][:], 1.0 / DM, None, OP.mult)
            V.tensor_mul(stat[3][:], stat[2][:], stat[2][:])
            vm = sV.inc(V.scalar_tensor_tensor(stat[4][:], stat[1][:], 1.0 / DM,
                                               stat[3][:], OP.mult, OP.subtract))
            sV.wait(SE, vm)
            SE.activation(stat[5][:], stat[4][:], AF.Ln, bias=cepsb[:])
            sm = sS.inc(SE.activation(stat[5][:], stat[5][:], AF.Exp, scale=-0.5))
            sS.wait(V, sm)
            sV.inc(V.tensor_mul(stat[3][:], stat[2][:], stat[5][:]))
            if tt >= 1:
                sV.wait(Y, xm[tt - 1])
            zl = sY.inc(Y.dma_start(out=zfull[:], in_=zsrc(tt)), 16)
            sY.wait(V, zl)
            if tt >= 1:
                sO.wait(V, x1st[tt - 1])
                if x1T_build:
                    sT.wait(V, x1trm[(tt - 1) * NDM + NDM - 1])
            xm[tt] = sV.inc(V.tensor_scalar(x1t[:], zfull[:],
                                            stat[5][:, 0:1], stat[3][:, 0:1],
                                            OP.mult, OP.subtract))
            sV.wait(Y, xm[tt])
            x1st[tt] = sO.inc(Y.dma_start(out=out_dst(tt), in_=x1t[:]), 16)
            if dbg_pair is not None and DBG_BUILD[0]:
                sO.inc(Y.dma_start(out=dbg_pair[0][tt * 128:(tt + 1) * 128, :],
                                   in_=zfull[:]), 16)
                x1st[tt] = sO.inc(Y.dma_start(out=dbg_pair[1][tt * 128:(tt + 1) * 128, :],
                                              in_=x1t[:]), 16)
            if x1T_build:
                for ic in range(NDM):
                    i = tr_i
                    tr_i += 1
                    sV.wait(TE, xm[tt])
                    if i >= 2:
                        sV.wait(TE, x1evm[i - 2])
                    x1trm[i] = sT.inc(TE.transpose(
                        pbank[6 + i % 2][:, 0:128],
                        x1t[:, ic * 128:(ic + 1) * 128], identf[:]))
                    sT.wait(V, x1trm[i])
                    x1evm[i] = sV.inc(V.tensor_copy(
                        x1T[:, ic * TC + tt * 128:ic * TC + (tt + 1) * 128],
                        pbank[6 + i % 2][:, 0:128]))
        if x1T_build:
            marks["x1Tv"] = sV.n
        marks["x1sO" if x1T_build else "outsO"] = sO.n
        sO.wait(Y)

    def phase3a(kind, eng):
        mm_part(
            kind, eng,
            rhs_src=lambda dw, fc: wod[fc * 128:(fc + 1) * 128, dw * 512:(dw + 1) * 512],
            lhs_sb=ctxT,
            bias_sb=bob_sb,
            res_src=lambda dw, tt: xck[tt * 128:(tt + 1) * 128, dw * 512:(dw + 1) * 512],
            zdst=lambda dw, tt: z1scr.ap()[tt * 128:(tt + 1) * 128,
                                           dw * 512:(dw + 1) * 512],
            lhs_wait="ctxT",
            agw_wait=False,
            markpfx="p3")

    run_phase(phase3a)
    stC.close()
    stX = ExitStack()
    x1T = sball(stX, "x1T", [128, NDM * TC], BF16)

    def phase3b(kind, eng):
        ln_part(kind, eng,
                zsrc=lambda tt: z1scr.ap()[tt * 128:(tt + 1) * 128, :],
                out_dst=lambda tt: x1scr.ap()[tt * 128:(tt + 1) * 128, :],
                dbg_pair=(dbg_t["z1"], dbg_t["x1"]) if DBG_BUILD[0] else None,
                x1T_build=True,
                markpfx="p3")

    def phase4a(kind, eng):
        mm_part(
            kind, eng,
            rhs_src=lambda dw, fc: agw_out.ap()[dw * DM + fc * 128:dw * DM + (fc + 1) * 128, :],
            lhs_sb=x1T,
            bias_sb=fbb_sb,
            res_src=lambda dw, tt: x1scr.ap()[tt * 128:(tt + 1) * 128,
                                              dw * 512:(dw + 1) * 512],
            zdst=lambda dw, tt: z2scr.ap()[tt * 128:(tt + 1) * 128,
                                           dw * 512:(dw + 1) * 512],
            lhs_wait=None,
            agw_wait=True,
            markpfx="p4")

    def phase4b(kind, eng):
        ln_part(kind, eng,
                zsrc=lambda tt: z2scr.ap()[tt * 128:(tt + 1) * 128, :],
                out_dst=lambda tt: out[tt * 128:(tt + 1) * 128, :],
                dbg_pair=None,
                x1T_build=False,
                markpfx="p4")

    run_phase(phase3b)
    run_phase(phase4a)
    run_phase(phase4b)
    stX.close()
    stW.close()
    st.close()
    return nc


_host_consts = {}


def _consts():
    if _host_consts:
        return _host_consts
    half = 64
    inv = 1.0 / (10000.0 ** (np.arange(half, dtype=np.float64) / half))
    ang = np.arange(S, dtype=np.float64)[None, :] * inv[:, None]
    _host_consts["cos"] = np.ascontiguousarray(np.cos(ang).astype(np.float32))
    _host_consts["sin"] = np.ascontiguousarray(np.sin(ang).astype(np.float32))
    cm = np.empty((128, 4 * 512), np.float32)
    for o in range(4):
        keep = (np.arange(512)[None, :] - np.arange(128)[:, None]) >= o * 128
        cm[:, o * 512:(o + 1) * 512] = np.where(keep, np.float32(0.0), np.float32(NEG))
    _host_consts["cm"] = cm
    return _host_consts


_nc_cache = {}


def _get_nc(dbg):
    key = bool(dbg)
    if key not in _nc_cache:
        DBG_BUILD[0] = key
        _nc_cache[key] = build_nc(dbg=key)
    return _nc_cache[key]


def make_in_maps(inputs):
    cst = _consts()
    x = np.ascontiguousarray(np.asarray(inputs["L0_attn_hidden_states"]).reshape(T, DM))
    Wq = np.asarray(inputs["L0_attn_Wq"])
    bq = np.asarray(inputs["L0_attn_bq"])
    Wk = np.asarray(inputs["L0_attn_Wk"])
    bk = np.asarray(inputs["L0_attn_bk"])
    Wv = np.asarray(inputs["L0_attn_Wv"])
    bv = np.asarray(inputs["L0_attn_bv"])
    Wo = np.ascontiguousarray(np.asarray(inputs["L0_attn_Wo"]))
    bo = np.asarray(inputs["L0_attn_bo"])
    pkf = np.asarray(inputs["L0_ffn_weight_packed"]).reshape(DM, 2048)
    scf = np.asarray(inputs["L0_ffn_weight_scales"]).reshape(DM, 32)
    fbv = np.asarray(inputs["L0_ffn_bias"])
    in_maps = []
    for c in range(8):
        in_maps.append({
            "xck": np.ascontiguousarray(x[c * TC:(c + 1) * TC]),
            "wqd": np.ascontiguousarray(Wq[:, c * QH:(c + 1) * QH]),
            "bqv": np.ascontiguousarray(bq[c * QH:(c + 1) * QH].reshape(R, 128).T),
            "wkd": np.ascontiguousarray(Wk[:, c * D:(c + 1) * D]),
            "bkv": np.ascontiguousarray(bk[c * D:(c + 1) * D].reshape(1, 128).T),
            "wvd": np.ascontiguousarray(Wv[:, c * D:(c + 1) * D]),
            "bvv": np.ascontiguousarray(bv[c * D:(c + 1) * D].reshape(1, 128).T),
            "wod": Wo,
            "bod": np.ascontiguousarray(bo.reshape(1, DM)),
            "pkd": np.ascontiguousarray(pkf[c * 512:(c + 1) * 512]),
            "scd": np.ascontiguousarray(scf[c * 512:(c + 1) * 512]),
            "fbd": np.ascontiguousarray(fbv.reshape(1, DM)),
            "cosd": cst["cos"],
            "sind": cst["sin"],
            "cmd": cst["cm"],
        })
    return in_maps


def kernel(**inputs):
    nc = _get_nc(False)
    in_maps = make_in_maps(inputs)
    res = run_bass_kernel_spmd(nc, in_maps, core_ids=list(range(8)))
    o = np.concatenate([res.results[c]["out"] for c in range(8)], axis=0)
    return np.ascontiguousarray(o.reshape(B, S, DM))


# version 04
# speedup vs baseline: 1.1196x; 1.1196x over previous
import numpy as np
import ml_dtypes
from contextlib import ExitStack

import concourse.bass as bass
import concourse.mybir as mybir
from concourse.bass_utils import run_bass_kernel_spmd
from concourse.aot_env import allow_ici_all_to_all

F32 = mybir.dt.float32
BF16 = mybir.dt.bfloat16
I32 = mybir.dt.int32
AF = mybir.ActivationFunctionType
OP = mybir.AluOpType

B, S, DM = 2, 4096, 4096
T = B * S
TC = T // 8
H, G, D, R = 32, 8, 128, 4
QH = R * D
SCALE = 1.0 / float(np.sqrt(np.float64(128.0)))
EXPB = -20.0
NEG = -3.0e38
EPS = 1e-5
NT = TC // 128
NDM = DM // 128
NG = S // 512

DBG_BUILD = [False]


class Px:
    def __init__(self):
        self.eng = None

    def __getattr__(self, name):
        if self.eng is not None:
            return getattr(self.eng, name)
        return lambda *a, **kw: None


class Sem:
    def __init__(self, h):
        self.h = h
        self.n = 0

    def inc(self, inst, k=1):
        if inst is not None:
            inst.then_inc(self.h, k)
        self.n += k
        return self.n

    def wait(self, px, val=None):
        if px.eng is not None:
            px.eng.wait_ge(self.h, self.n if val is None else val)


def build_nc(dbg=False):
    nc = bass.Bass("TRN2", target_bir_lowering=False, debug=False)

    xck = nc.dram_tensor("xck", [TC, DM], F32, kind="ExternalInput").ap()
    wqd = nc.dram_tensor("wqd", [DM, QH], F32, kind="ExternalInput").ap()
    bqv = nc.dram_tensor("bqv", [128, R], F32, kind="ExternalInput").ap()
    wkd = nc.dram_tensor("wkd", [DM, D], F32, kind="ExternalInput").ap()
    bkv = nc.dram_tensor("bkv", [128, 1], F32, kind="ExternalInput").ap()
    wvd = nc.dram_tensor("wvd", [DM, D], F32, kind="ExternalInput").ap()
    bvv = nc.dram_tensor("bvv", [128, 1], F32, kind="ExternalInput").ap()
    wod = nc.dram_tensor("wod", [DM, DM], F32, kind="ExternalInput").ap()
    bod = nc.dram_tensor("bod", [1, DM], F32, kind="ExternalInput").ap()
    pkd = nc.dram_tensor("pkd", [512, 2048], I32, kind="ExternalInput").ap()
    scd = nc.dram_tensor("scd", [512, 32], F32, kind="ExternalInput").ap()
    fbd = nc.dram_tensor("fbd", [1, DM], F32, kind="ExternalInput").ap()
    cosd = nc.dram_tensor("cosd", [64, S], F32, kind="ExternalInput").ap()
    sind = nc.dram_tensor("sind", [64, S], F32, kind="ExternalInput").ap()
    cmd = nc.dram_tensor("cmd", [128, 4 * 512], F32, kind="ExternalInput").ap()
    out = nc.dram_tensor("out", [TC, DM], F32, kind="ExternalOutput").ap()

    agx_in = nc.dram_tensor("agx_in", [DM, TC], BF16)
    agx_out = nc.dram_tensor("agx_out", [8 * DM, TC], BF16)
    agw_in = nc.dram_tensor("agw_in", [DM, 512], BF16)
    agw_out = nc.dram_tensor("agw_out", [8 * DM, 512], BF16)
    a2a_in = nc.dram_tensor("a2a_in", [8, QH, TC], BF16)
    a2a_out = nc.dram_tensor("a2a_out", [8, QH, TC], BF16)
    z1scr = nc.dram_tensor("z1scr", [TC, DM], F32)
    x1scr = nc.dram_tensor("x1scr", [TC, DM], F32)
    z2scr = nc.dram_tensor("z2scr", [TC, DM], F32)

    dbg_t = {}
    if dbg:
        dbg_t["qT"] = nc.dram_tensor("d_qT", [128, R * S], F32, kind="ExternalOutput").ap()
        dbg_t["kT"] = nc.dram_tensor("d_kT", [128, T], F32, kind="ExternalOutput").ap()
        dbg_t["vb"] = nc.dram_tensor("d_vb", [128, T], F32, kind="ExternalOutput").ap()
        dbg_t["ctx"] = nc.dram_tensor("d_ctx", [8 * QH, TC], F32, kind="ExternalOutput").ap()
        dbg_t["z1"] = nc.dram_tensor("d_z1", [TC, DM], F32, kind="ExternalOutput").ap()
        dbg_t["x1"] = nc.dram_tensor("d_x1", [TC, DM], F32, kind="ExternalOutput").ap()
        dbg_t["wt"] = nc.dram_tensor("d_wt", [DM, 512], F32, kind="ExternalOutput").ap()
        dbg_t["win"] = nc.dram_tensor("d_win", [DM, 512], F32, kind="ExternalOutput").ap()
        dbg_t["wtile0"] = nc.dram_tensor("d_wtile0", [128, DM], F32, kind="ExternalOutput").ap()
        dbg_t["wtile1"] = nc.dram_tensor("d_wtile1", [128, DM], F32, kind="ExternalOutput").ap()

    st = ExitStack()

    def sball(stk, nm, sh, dt):
        return stk.enter_context(nc.sbuf_tensor(nm, sh, dt))

    sY = Sem(st.enter_context(nc.semaphore("sY")))
    sG = Sem(st.enter_context(nc.semaphore("sG")))
    sD = Sem(st.enter_context(nc.semaphore("sD")))
    sT = Sem(st.enter_context(nc.semaphore("sT")))
    sV = Sem(st.enter_context(nc.semaphore("sV")))
    sS = Sem(st.enter_context(nc.semaphore("sS")))
    sC = Sem(st.enter_context(nc.semaphore("sC")))
    sO = Sem(st.enter_context(nc.semaphore("sO")))
    s_init = Sem(st.enter_context(nc.semaphore("s_init")))
    s_bias = Sem(st.enter_context(nc.semaphore("s_bias")))
    s_cm = Sem(st.enter_context(nc.semaphore("s_cm")))
    s_tab = Sem(st.enter_context(nc.semaphore("s_tab")))
    s_ct = Sem(st.enter_context(nc.semaphore("s_ct")))
    s_st = [Sem(st.enter_context(nc.semaphore("s_st%d" % i))) for i in range(4)]
    s_xf = [Sem(st.enter_context(nc.semaphore("s_xf%d" % i))) for i in range(2)]
    s_xg = [Sem(st.enter_context(nc.semaphore("s_xg%d" % i))) for i in range(2)]
    s_qd = [Sem(st.enter_context(nc.semaphore("s_qd%d" % i))) for i in range(2)]
    s_o2 = [Sem(st.enter_context(nc.semaphore("s_o2%d" % i))) for i in range(2)]
    s_wo = [Sem(st.enter_context(nc.semaphore("s_wo%d" % i))) for i in range(2)]
    s_xr = [Sem(st.enter_context(nc.semaphore("s_xr%d" % i))) for i in range(2)]
    s_zs = [Sem(st.enter_context(nc.semaphore("s_zs%d" % i))) for i in range(2)]
    all_sems = ([sY, sG, sD, sT, sV, sS, sC, sO, s_init, s_bias, s_cm, s_tab, s_ct]
                + s_st + s_xf + s_xg + s_qd + s_o2 + s_wo + s_xr + s_zs)

    pbank = [st.enter_context(nc.psum_tensor("pb%d" % i, [128, 512], F32))
             for i in range(8)]

    identf = sball(st, "identf", [128, 128], F32)
    io_a = sball(st, "io_a", [128, 128], I32)
    io_b = sball(st, "io_b", [128, 128], I32)
    onecol = sball(st, "onecol", [128, 1], BF16)
    onerow = sball(st, "onerow", [1, 128], F32)
    cexpb = sball(st, "cexpb", [128, 1], F32)
    cepsb = sball(st, "cepsb", [128, 1], F32)
    bq_sb = sball(st, "bq_sb", [128, R], F32)
    bk_sb = sball(st, "bk_sb", [128, 1], F32)
    bv_sb = sball(st, "bv_sb", [128, 1], F32)

    def run_phase(fn):
        snap = [s_.n for s_ in all_sems]

        def mk(kind):
            def f(eng):
                for s_, v in zip(all_sems, snap):
                    s_.n = v
                fn(kind, eng)
            return f

        blk = nc.Block()
        blk.__enter__()
        blk.gpsimd(mk('g'))
        blk.vector(mk('v'))
        blk.scalar(mk('s'))
        blk.tensor(mk('t'))
        blk.sync(mk('y'))
        blk.__exit__(None, None, None)

    marks = {}

    ph0 = ExitStack()
    xf32 = [sball(ph0, "xf%d" % i, [128, DM], F32) for i in range(2)]
    xT128 = [sball(ph0, "xT%d" % i, [128, 128], BF16) for i in range(4)]
    pkt = sball(ph0, "pkt", [128, 2048], I32)
    sct = sball(ph0, "sct", [128, 32], F32)
    d_n = sball(ph0, "d_n", [128, 1024], I32)
    d_s = sball(ph0, "d_s", [128, 1024], I32)
    d_i = sball(ph0, "d_i", [128, 1024], I32)
    d_e = sball(ph0, "d_e", [128, 1024], I32)
    d_m = sball(ph0, "d_m", [128, 1024], I32)
    d_a = sball(ph0, "d_a", [128, 1024], F32)
    d_u = sball(ph0, "d_u", [128, 1024], F32)
    d_q = sball(ph0, "d_q", [128, 1024], F32)
    d_t = sball(ph0, "d_t", [128, 1024], F32)
    d_g = sball(ph0, "d_g", [128, 1024], F32)
    wtile = sball(ph0, "wtile", [128, DM], F32)
    wT128 = [sball(ph0, "wT%d" % i, [128, 128], BF16) for i in range(4)]

    def phase0(kind, eng):
        PXs = {k: Px() for k in "gvsty"}
        PXs[kind].eng = eng
        GE, V, SE, TE, Y = PXs['g'], PXs['v'], PXs['s'], PXs['t'], PXs['y']

        GE.memset(onecol[:], 1.0)
        GE.memset(onerow[:], 1.0)
        GE.memset(cexpb[:], EXPB)
        GE.memset(cepsb[:], EPS)
        GE.iota(io_a[:], [[1, 128]], base=0, channel_multiplier=0,
                allow_small_or_imprecise_dtypes=True)
        s_init.inc(GE.iota(io_b[:], [[0, 128]], base=0, channel_multiplier=1,
                           allow_small_or_imprecise_dtypes=True))
        s_bias.inc(GE.dma_start(out=bq_sb[:], in_=bqv), 16)
        s_bias.inc(GE.dma_start(out=bk_sb[:], in_=bkv), 16)
        s_bias.inc(GE.dma_start(out=bv_sb[:], in_=bvv), 16)
        s_init.wait(V)
        idm = sV.inc(V.tensor_tensor(identf[:], io_a[:], io_b[:], OP.is_equal))
        sV.wait(TE, idm)

        trm = {}
        evm = {}
        stm = {}
        for t in range(NT):
            if t >= 2:
                sT.wait(Y, trm[(t - 2) * NDM + NDM - 1])
            ldx = s_xf[t % 2].inc(Y.dma_start(out=xf32[t % 2][:],
                                              in_=xck[t * 128:(t + 1) * 128, :]), 16)
            for db in range(NDM):
                i = t * NDM + db
                s_xf[t % 2].wait(TE, ldx)
                if i >= 4:
                    sV.wait(TE, evm[i - 4])
                trm[i] = sT.inc(TE.transpose(
                    pbank[4 + i % 4][:, 0:128],
                    xf32[t % 2][:, db * 128:(db + 1) * 128], identf[:]))
                sT.wait(V, trm[i])
                if i >= 4:
                    s_st[i % 4].wait(V, stm[i - 4])
                evm[i] = sV.inc(V.tensor_copy(xT128[i % 4][:], pbank[4 + i % 4][:, 0:128]))
                sV.wait(Y, evm[i])
                stm[i] = s_st[i % 4].inc(Y.dma_start(
                    out=agx_in.ap()[db * 128:(db + 1) * 128, t * 128:(t + 1) * 128],
                    in_=xT128[i % 4][:]), 16)
        for s_ in s_st:
            s_.wait(GE)
        marks["agx"] = sC.inc(GE.collective_compute(
            "AllGather", OP.bypass, replica_groups=[list(range(8))],
            ins=[agx_in.ap().opt()], outs=[agx_out.ap().opt()]))

        wtrm = {}
        wevm = {}
        wstm = {}
        vdeq = {}
        for ot in range(4):
            if ot >= 1:
                sV.wait(GE, vdeq[ot - 1])
            lp = sG.inc(GE.dma_start(out=pkt[:], in_=pkd[ot * 128:(ot + 1) * 128, :]), 16)
            lp = sG.inc(GE.dma_start(out=sct[:], in_=scd[ot * 128:(ot + 1) * 128, :]), 16)
            sG.wait(V, lp)
            if ot >= 1:
                sT.wait(V, wtrm[(ot - 1) * NDM + NDM - 1])
            for half in range(2):
                co = half * 1024
                pslice = pkt[:, co:co + 1024]
                for hi in range(2):
                    if hi == 0:
                        V.tensor_single_scalar(d_n[:], pslice, 4, OP.logical_shift_right)
                    else:
                        V.tensor_single_scalar(d_n[:], pslice, 15, OP.bitwise_and)
                    V.tensor_single_scalar(d_s[:], d_n[:], 3, OP.logical_shift_right)
                    V.tensor_single_scalar(d_i[:], d_n[:], 7, OP.bitwise_and)
                    V.tensor_single_scalar(d_e[:], d_i[:], 1, OP.logical_shift_right)
                    V.tensor_single_scalar(d_m[:], d_i[:], 1, OP.bitwise_and)
                    V.tensor_scalar(d_a[:], d_e[:], 2.0, 2.0, OP.mult, OP.min)
                    V.tensor_scalar(d_u[:], d_e[:], 0.5, 0.5, OP.mult, OP.max)
                    V.tensor_scalar_add(d_u[:], d_u[:], -0.25)
                    V.tensor_mul(d_q[:], d_u[:], d_u[:])
                    V.scalar_tensor_tensor(d_t[:], d_m[:], 1.0, d_a[:], OP.mult, OP.add)
                    V.scalar_tensor_tensor(d_g[:], d_q[:], 0.4375, d_t[:], OP.add, OP.mult)
                    V.tensor_scalar(d_t[:], d_s[:], -2.0, 1.0, OP.mult, OP.add)
                    V.tensor_mul(d_g[:], d_g[:], d_t[:])
                    ssrc = bass.AP(sct, half * 16, [[32, 128], [1, 16], [0, 64]])
                    dsts = bass.AP(wtile, 2 * co + hi, [[DM, 128], [2, 1024]])
                    V.tensor_tensor(dsts, d_g[:], ssrc, OP.mult)
            vdeq[ot] = sV.inc(V.engine_nop())
            if DBG_BUILD[0] and ot == 0:
                sV.wait(GE, vdeq[0])
                sO.inc(GE.dma_start(out=dbg_t["wtile0"], in_=wtile[:]), 16)
            for ib in range(NDM):
                i = ot * NDM + ib
                sV.wait(TE, vdeq[ot])
                if i >= 4:
                    sV.wait(TE, wevm[i - 4])
                wtrm[i] = sT.inc(TE.transpose(
                    pbank[4 + i % 4][:, 0:128],
                    wtile[:, ib * 128:(ib + 1) * 128], identf[:]))
                sT.wait(V, wtrm[i])
                if i >= 4:
                    s_st[i % 4].wait(V, wstm[i - 4])
                wevm[i] = sV.inc(V.tensor_copy(wT128[i % 4][:], pbank[4 + i % 4][:, 0:128]))
                sV.wait(Y, wevm[i])
                wstm[i] = s_st[i % 4].inc(Y.dma_start(
                    out=agw_in.ap()[ib * 128:(ib + 1) * 128, ot * 128:(ot + 1) * 128],
                    in_=wT128[i % 4][:]), 16)
        if DBG_BUILD[0]:
            sT.wait(GE, wtrm[NDM - 1])
            sO.inc(GE.dma_start(out=dbg_t["wtile1"], in_=wtile[:]), 16)
        for s_ in s_st:
            s_.wait(GE)
        sC.wait(GE, marks["agx"])
        marks["agw"] = sC.inc(GE.collective_compute(
            "AllGather", OP.bypass, replica_groups=[list(range(8))],
            ins=[agw_in.ap().opt()], outs=[agw_out.ap().opt()]))

    run_phase(phase0)
    ph0.close()

    stA = ExitStack()
    wq_sb = sball(stA, "wq_sb", [128, NDM * QH], BF16)
    wk_sb = sball(stA, "wk_sb", [128, NDM * D], BF16)
    wv_sb = sball(stA, "wv_sb", [128, NDM * D], BF16)
    csl = [sball(stA, "csl0", [64, 512], F32)]
    snl = [sball(stA, "snl0", [64, 512], F32)]
    cm_sb = sball(stA, "cm_sb", [128, 4 * 512], BF16)
    qT_sb = sball(stA, "qT_sb", [128, R * S], BF16)
    kT_sb = sball(stA, "kT_sb", [128, T], BF16)
    vb_sb = sball(stA, "vb_sb", [128, T], BF16)
    xg = [sball(stA, "xg%d" % i, [128, 16, 512], BF16) for i in range(2)]
    qa = [sball(stA, "qa%d" % i, [128, 512], F32) for i in range(2)]
    qb = [sball(stA, "qb%d" % i, [64, 512], F32) for i in range(2)]
    rp = [sball(stA, "rp%d" % i, [64, 512], F32) for i in range(2)]
    o2s = [sball(stA, "o2s%d" % i, [64, 512], BF16) for i in range(2)]
    vst = sball(stA, "vst", [128, 512], F32)
    prT = sball(stA, "prT", [128, 32 * 512], BF16)
    dstg = [sball(stA, "dstg0", [128, 512], F32)]
    ctxe = [sball(stA, "ctxe0", [128, 512], BF16)]

    def phase1(b):
        def fn(kind, eng):
            PXs = {k: Px() for k in "gvsty"}
            PXs[kind].eng = eng
            GE, V, SE, TE, Y = PXs['g'], PXs['v'], PXs['s'], PXs['t'], PXs['y']

            if b == 0:
                for dmc in range(NDM):
                    sG.inc(GE.dma_start(out=wq_sb[:, dmc * QH:(dmc + 1) * QH],
                                        in_=wqd[dmc * 128:(dmc + 1) * 128, :]), 16)
                    sG.inc(GE.dma_start(out=wk_sb[:, dmc * D:(dmc + 1) * D],
                                        in_=wkd[dmc * 128:(dmc + 1) * 128, :]), 16)
                    sG.inc(GE.dma_start(out=wv_sb[:, dmc * D:(dmc + 1) * D],
                                        in_=wvd[dmc * 128:(dmc + 1) * 128, :]), 16)
                s_cm.inc(GE.dma_start(out=cm_sb[:], in_=cmd), 16)
                marks["wqkv"] = sG.n
                marks["cmld"] = s_cm.n

            mmhalf = {}
            ldm = {}
            evfb = {}
            ropev = {}
            roped = {}
            qadma = {}
            vtr = {}
            vev = {}
            for tg in range(8):
                gtok = b * S + tg * 512
                rank = gtok // TC
                coff = gtok % TC
                cl = tg * 512
                if tg >= 1:
                    sV.wait(Y, ropev[(tg - 1) * 5 + 4])
                s_tab.inc(Y.dma_start(out=csl[0][:], in_=cosd[:, cl:cl + 512]), 16)
                tabm = s_tab.inc(Y.dma_start(out=snl[0][:], in_=sind[:, cl:cl + 512]), 16)
                for half in range(2):
                    if tg > 0:
                        sT.wait(Y, mmhalf[(tg - 1, half)])
                    lm = 0
                    sC.wait(Y, marks["agx"])
                    for j in range(16):
                        dmc = half * 16 + j
                        lm = s_xg[half].inc(Y.dma_start(
                            out=xg[half][:, j, :],
                            in_=agx_out.ap()[rank * DM + dmc * 128:rank * DM + dmc * 128 + 128,
                                             coff:coff + 512]), 16)
                    ldm[(tg, half)] = lm
                sG.wait(TE, marks["wqkv"])
                for half in range(2):
                    s_xg[half].wait(TE, ldm[(tg, half)])
                    for j in range(16):
                        dmc = half * 16 + j
                        first = (dmc == 0)
                        last = (dmc == NDM - 1)
                        if first and tg > 0:
                            for bkk in range(6):
                                sV.wait(TE, evfb[(tg - 1, bkk)])
                        TE.matmul(pbank[0][:, 0:512],
                                  wq_sb[:, dmc * QH + 0 * 128:dmc * QH + 1 * 128],
                                  xg[half][:, j, :], start=first, stop=last)
                        TE.matmul(pbank[1][:, 0:512],
                                  wq_sb[:, dmc * QH + 1 * 128:dmc * QH + 2 * 128],
                                  xg[half][:, j, :], start=first, stop=last)
                        TE.matmul(pbank[2][:, 0:512],
                                  wq_sb[:, dmc * QH + 2 * 128:dmc * QH + 3 * 128],
                                  xg[half][:, j, :], start=first, stop=last)
                        TE.matmul(pbank[3][:, 0:512],
                                  wq_sb[:, dmc * QH + 3 * 128:dmc * QH + 4 * 128],
                                  xg[half][:, j, :], start=first, stop=last)
                        TE.matmul(pbank[4][:, 0:512],
                                  wk_sb[:, dmc * D:(dmc + 1) * D],
                                  xg[half][:, j, :], start=first, stop=last)
                        inst = TE.matmul(pbank[5][:, 0:512],
                                         wv_sb[:, dmc * D:(dmc + 1) * D],
                                         xg[half][:, j, :], start=first, stop=last)
                        if j == 15:
                            mmhalf[(tg, half)] = sT.inc(inst)
                mmend = mmhalf[(tg, 1)]
                sT.wait(V, mmend)
                if b == 0 and tg == 0:
                    s_bias.wait(V, 48)
                s_tab.wait(V, tabm)
                cs = csl[0][:]
                sn = snl[0][:]
                for fb in range(5):
                    qi = fb % 2
                    tix = tg * 5 + fb
                    bias = bq_sb[:, fb:fb + 1] if fb < 4 else bk_sb[:, 0:1]
                    prev_qi = tix - {0: 1, 1: 3, 2: 2, 3: 2, 4: 2}[fb]
                    if prev_qi >= 0:
                        s_qd[qi].wait(V, qadma.get(prev_qi, 0) or 0)
                    evfb[(tg, fb)] = sV.inc(V.tensor_scalar(
                        qa[qi][:], pbank[fb][:, 0:512], bias, None, OP.add))
                    sV.wait(SE, evfb[(tg, fb)])
                    if prev_qi >= 0:
                        sV.wait(SE, ropev.get(prev_qi, 0) or 0)
                    qadma[tix] = s_qd[qi].inc(SE.dma_start(out=qb[qi][:], in_=qa[qi][64:128, :]), 16)
                    s_qd[qi].wait(V, qadma[tix])
                    dst_lo = (qT_sb[0:64, fb * S + cl:fb * S + cl + 512] if fb < 4
                              else kT_sb[0:64, gtok:gtok + 512])
                    V.tensor_tensor(rp[0][:], qa[qi][0:64, :], cs, OP.mult)
                    V.tensor_tensor(rp[1][:], qb[qi][:], sn, OP.mult)
                    V.tensor_sub(dst_lo, rp[0][:], rp[1][:])
                    V.tensor_tensor(rp[0][:], qa[qi][0:64, :], sn, OP.mult)
                    V.tensor_tensor(rp[1][:], qb[qi][:], cs, OP.mult)
                    if tix >= 2:
                        s_o2[tix % 2].wait(V, roped.get(tix - 2, 0) or 0)
                    ropev[tix] = sV.inc(V.tensor_add(o2s[tix % 2][:], rp[0][:], rp[1][:]))
                    sV.wait(SE, ropev[tix])
                    if fb < 4:
                        dst_hi = qT_sb[64:128, fb * S + cl:fb * S + cl + 512]
                    else:
                        dst_hi = kT_sb[64:128, gtok:gtok + 512]
                    roped[tix] = s_o2[tix % 2].inc(SE.dma_start(out=dst_hi, in_=o2s[tix % 2][:]), 16)
                if tg > 0:
                    sT.wait(V, vtr[(tg - 1) * 4 + 3])
                evfb[(tg, 5)] = sV.inc(V.tensor_scalar(
                    vst[:], pbank[5][:, 0:512], bv_sb[:, 0:1], None, OP.add))
                for j in range(4):
                    i = tg * 4 + j
                    sV.wait(TE, evfb[(tg, 5)])
                    if i >= 2:
                        sV.wait(TE, vev[i - 2])
                    vtr[i] = sT.inc(TE.transpose(pbank[6 + i % 2][:, 0:128],
                                                 vst[:, j * 128:(j + 1) * 128], identf[:]))
                    sT.wait(V, vtr[i])
                    vev[i] = sV.inc(V.tensor_copy(
                        vb_sb[:, gtok + j * 128:gtok + (j + 1) * 128],
                        pbank[6 + i % 2][:, 0:128]))
            marks["ropeD%d" % b] = (s_o2[0].n, s_o2[1].n, s_qd[0].n, s_qd[1].n)
        return fn

    def phase2(b):
        def fn(kind, eng):
            PXs = {k: Px() for k in "gvsty"}
            PXs[kind].eng = eng
            GE, V, SE, TE, Y = PXs['g'], PXs['v'], PXs['s'], PXs['t'], PXs['y']
            boff = b * S
            m0, m1, m2, m3 = marks["ropeD%d" % b]
            s_o2[0].wait(TE, m0)
            s_o2[1].wait(TE, m1)
            s_qd[0].wait(TE, m2)
            s_qd[1].wait(TE, m3)
            s_cm.wait(V, marks["cmld"])

            psfree = {0: None, 1: None}
            prread = {}
            dnfree = None
            bcfree = None
            cxfree = None
            ctfree = {0: None}
            dgfree = {0: None}

            for grp in range(NG):
                nkb = 4 * (grp + 1)
                gq = boff + grp * 512
                dest = gq // TC
                qoff = gq % TC
                for h in range(4):
                    expm = {}
                    dn_in = None
                    pv_in = None
                    for kb in range(nkb + 2):
                        if kb < nkb:
                            slot = kb % 2
                            if psfree[slot] is not None:
                                w, val = psfree[slot]
                                (sS if w == 's' else sV).wait(TE, val)
                            mmm = sT.inc(TE.matmul(
                                pbank[slot][:, 0:512],
                                kT_sb[:, boff + kb * 128:boff + (kb + 1) * 128],
                                qT_sb[:, h * S + grp * 512:h * S + (grp + 1) * 512],
                                start=True, stop=True))
                            diag = kb >= nkb - 4
                            if not diag:
                                sT.wait(SE, mmm)
                                if prread.get(kb) is not None:
                                    sT.wait(SE, prread[kb])
                                expm[kb] = sS.inc(SE.activation(
                                    prT[:, kb * 512:(kb + 1) * 512], pbank[slot][:, 0:512],
                                    AF.Exp, bias=cexpb[:], scale=SCALE))
                                psfree[slot] = ('s', expm[kb])
                            else:
                                oi = kb - (nkb - 4)
                                sT.wait(V, mmm)
                                if dgfree[0] is not None:
                                    sS.wait(V, dgfree[0])
                                stm_ = sV.inc(V.scalar_tensor_tensor(
                                    dstg[0][:], pbank[slot][:, 0:512], SCALE,
                                    cm_sb[:, oi * 512:(oi + 1) * 512], OP.mult, OP.add))
                                psfree[slot] = ('v', stm_)
                                sV.wait(SE, stm_)
                                if prread.get(kb) is not None:
                                    sT.wait(SE, prread[kb])
                                expm[kb] = sS.inc(SE.activation(
                                    prT[:, kb * 512:(kb + 1) * 512], dstg[0][:],
                                    AF.Exp, bias=cexpb[:], scale=1.0))
                                dgfree[0] = expm[kb]
                        kd = kb - 2
                        if 0 <= kd < nkb:
                            sS.wait(TE, expm[kd])
                            if kd == 0:
                                if dnfree is not None:
                                    sV.wait(TE, dnfree)
                                if cxfree is not None:
                                    sV.wait(TE, cxfree)
                            dn_in = TE.matmul(pbank[2][0:1, 0:512], onecol[:],
                                              prT[:, kd * 512:(kd + 1) * 512],
                                              start=(kd == 0), stop=(kd == nkb - 1))
                            pv_in = TE.matmul(pbank[4][:, 0:512],
                                              vb_sb[:, boff + kd * 128:boff + (kd + 1) * 128],
                                              prT[:, kd * 512:(kd + 1) * 512],
                                              start=(kd == 0), stop=(kd == nkb - 1))
                            prread[kd] = sT.inc(pv_in)
                    em = sT.n
                    sT.wait(V, em)
                    V.tensor_copy(rp[0][0:1, :], pbank[2][0:1, 0:512])
                    dn2 = sV.inc(V.reciprocal(rp[1][0:1, :], rp[0][0:1, :]))
                    dnfree = dn2
                    if bcfree is not None:
                        sV.wait(TE, bcfree)
                    sV.wait(TE, dn2)
                    dbm = sT.inc(TE.matmul(pbank[3][:, 0:512], onerow[:],
                                           rp[1][0:1, :], start=True, stop=True))
                    ci = 0
                    sT.wait(V, dbm)
                    bcm = sV.inc(V.tensor_copy(vst[:, 0:512], pbank[3][:, 0:512]))
                    bcfree = bcm
                    if ctfree[ci] is not None:
                        sY.wait(V, ctfree[ci])
                    fm = sV.inc(V.tensor_tensor(ctxe[ci][:], pbank[4][:, 0:512],
                                                vst[:, 0:512], OP.mult))
                    cxfree = fm
                    sV.wait(Y, fm)
                    ctfree[ci] = sY.inc(Y.dma_start(
                        out=a2a_in.ap()[dest, h * 128:(h + 1) * 128, qoff:qoff + 512],
                        in_=ctxe[ci][:]), 16)
        return fn

    run_phase(phase1(0))
    run_phase(phase2(0))
    run_phase(phase1(1))
    run_phase(phase2(1))

    def phase_a2a(kind, eng):
        PXs = {k: Px() for k in "gvsty"}
        PXs[kind].eng = eng
        GE = PXs['g']
        sY.wait(GE)
        with allow_ici_all_to_all():
            marks["a2a"] = sC.inc(GE.collective_compute(
                "AllToAll", OP.bypass, replica_groups=[list(range(8))],
                ins=[a2a_in.ap().opt()], outs=[a2a_out.ap().opt()]))
        if DBG_BUILD[0]:
            GE.wait_ge(sC.h, marks["a2a"])
            sO.inc(GE.dma_start(out=dbg_t["ctx"],
                                in_=a2a_out.ap().rearrange("a b c -> (a b) c")), 16)
            sO.inc(GE.dma_start(out=dbg_t["qT"], in_=qT_sb[:]), 16)
            sO.inc(GE.dma_start(out=dbg_t["kT"], in_=kT_sb[:]), 16)
            sO.inc(GE.dma_start(out=dbg_t["vb"], in_=vb_sb[:]), 16)
            sO.inc(GE.dma_start(out=dbg_t["wt"], in_=agw_out.ap()[0:DM, :]), 16)
            sO.inc(GE.dma_start(out=dbg_t["win"], in_=agw_in.ap()), 16)
            sO.wait(GE)

    run_phase(phase_a2a)
    stA.close()

    stW = ExitStack()
    woh = sball(stW, "woh", [128, NDM, 512], BF16)
    bob_sb = sball(stW, "bob_sb", [128, DM], F32)
    fbb_sb = sball(stW, "fbb_sb", [128, DM], F32)
    zc = [sball(stW, "zc%d" % i, [128, 512], F32) for i in range(2)]
    zd = [sball(stW, "zd%d" % i, [128, 512], F32) for i in range(2)]
    sqd = sball(stW, "sqd", [128, 512], F32)
    xres = [sball(stW, "xres%d" % i, [128, 512], F32) for i in range(2)]
    sums = sball(stW, "sums", [128, NT * 8], F32)
    sqs = sball(stW, "sqs", [128, NT * 8], F32)
    stat = [sball(stW, "stat%d" % i, [128, 128], F32) for i in range(6)]
    r1c = sball(stW, "r1c", [128, 1], F32)
    r2c = sball(stW, "r2c", [128, 1], F32)
    zfull = sball(stW, "zfull", [128, DM], F32)
    x1t = sball(stW, "x1t", [128, DM], F32)

    stC = ExitStack()
    ctxT = sball(stC, "ctxT", [128, NDM * TC], BF16)
    bf_row = sball(stC, "bf_row", [1, DM], F32)

    def mm_part(kind, eng, *, rhs_src, lhs_sb, bias_sb, res_src, zdst,
                lhs_wait, agw_wait, markpfx):
        PXs = {k: Px() for k in "gvsty"}
        PXs[kind].eng = eng
        GE, V, SE, TE, Y = PXs['g'], PXs['v'], PXs['s'], PXs['t'], PXs['y']

        if markpfx == "p3":
            sC.wait(Y, marks["a2a"])
            lm = 0
            a2f = a2a_out.ap().rearrange("a b c -> (a b) c")
            for fc in range(NDM):
                lm = s_ct.inc(Y.dma_start(out=ctxT[:, fc * TC:(fc + 1) * TC],
                                          in_=a2f[fc * 128:(fc + 1) * 128, :]), 16)
            marks["ctxT"] = lm
            for (src_row, dstsb) in ((bod, bob_sb), (fbd, fbb_sb)):
                bfl = s_bias.inc(GE.dma_start(out=bf_row[:], in_=src_row), 16)
                s_bias.wait(TE, bfl)
                mlast = None
                for w in range(8):
                    m1 = sT.inc(TE.matmul(pbank[2][:, 0:512], onerow[:],
                                          bf_row[0:1, w * 512:(w + 1) * 512],
                                          start=True, stop=True))
                    sT.wait(V, m1)
                    m2 = sV.inc(V.tensor_copy(dstsb[:, w * 512:(w + 1) * 512],
                                              pbank[2][:, 0:512]))
                    sV.wait(TE, m2)
                    mlast = m1
                sT.wait(GE, mlast)
            marks["bcb"] = sV.n
        else:
            sO.wait(Y, marks["x1sO"])

        wldm = {}
        mmend = {}
        e2m = {}
        sqm = {}
        stm_ = {}
        xldm = {}
        for dw in range(8):
            for half in range(2):
                if dw > 0:
                    sT.wait(GE, mmend[(dw - 1, half)])
                if agw_wait:
                    sC.wait(GE, marks["agw"])
                lm = 0
                for fc in range(half * 16, half * 16 + 16):
                    lm = s_wo[half].inc(GE.dma_start(out=woh[:, fc, :], in_=rhs_src(dw, fc)), 16)
                wldm[(dw, half)] = lm
            for tt in range(NT):
                bk = tt % 2
                i34 = dw * NT + tt
                if i34 >= 2:
                    sV.wait(Y, e2m[i34 - 2])
                xldm[i34] = s_xr[bk].inc(Y.dma_start(out=xres[bk][:], in_=res_src(dw, tt)), 16)
                if lhs_wait == "ctxT":
                    s_ct.wait(TE, marks["ctxT"])
                if i34 >= 2:
                    sV.wait(TE, e2m[i34 - 2])
                for half in range(2):
                    s_wo[half].wait(TE, wldm[(dw, half)])
                    inst = None
                    for fc in range(half * 16, half * 16 + 16):
                        inst = TE.matmul(pbank[bk][:, 0:512],
                                         lhs_sb[:, fc * TC + tt * 128:fc * TC + (tt + 1) * 128],
                                         woh[:, fc, :],
                                         start=(fc == 0), stop=(fc == NDM - 1))
                    mmend[(dw, half)] = sT.inc(inst)
                m = mmend[(dw, 1)]
                sT.wait(V, m)
                if markpfx == "p3":
                    sV.wait(V, marks["bcb"])
                s_xr[bk].wait(V, xldm[i34])
                if i34 >= 2:
                    sS.wait(V, sqm[i34 - 2])
                    s_zs[bk].wait(V, stm_[i34 - 2])
                V.tensor_tensor(zc[bk][:], pbank[bk][:, 0:512],
                                bias_sb[:, dw * 512:(dw + 1) * 512], OP.add)
                e2m[i34] = sV.inc(V.scalar_tensor_tensor(
                    zd[bk][:], zc[bk][:], 1.0, xres[bk][:], OP.mult, OP.add,
                    accum_out=sums[:, tt * 8 + dw:tt * 8 + dw + 1]))
                sV.wait(SE, e2m[i34])
                sqm[i34] = sS.inc(SE.activation(
                    sqd[:], zd[bk][:], AF.Square,
                    accum_out=sqs[:, tt * 8 + dw:tt * 8 + dw + 1]))
                sS.wait(Y, sqm[i34])
                stm_[i34] = s_zs[bk].inc(Y.dma_start(out=zdst(dw, tt), in_=zd[bk][:]), 16)
        marks[markpfx + "sS"] = sS.n
        marks[markpfx + "sO"] = (s_zs[0].n, s_zs[1].n)

    def ln_part(kind, eng, *, zsrc, out_dst, dbg_pair, x1T_build, markpfx):
        PXs = {k: Px() for k in "gvsty"}
        PXs[kind].eng = eng
        GE, V, SE, TE, Y = PXs['g'], PXs['v'], PXs['s'], PXs['t'], PXs['y']
        xm = {}
        x1trm = {}
        x1evm = {}
        x1st = {}
        tr_i = 0
        z0, z1_ = marks[markpfx + "sO"]
        s_zs[0].wait(Y, z0)
        s_zs[1].wait(Y, z1_)
        for tt in range(NT):
            s_zs[0].wait(V, z0)
            s_zs[1].wait(V, z1_)
            sS.wait(V, marks[markpfx + "sS"])
            V.tensor_reduce(r1c[:], sums[:, tt * 8:tt * 8 + 8], mybir.AxisListType.X, OP.add)
            V.tensor_reduce(r2c[:], sqs[:, tt * 8:tt * 8 + 8], mybir.AxisListType.X, OP.add)
            V.tensor_copy(stat[0][:], bass.AP(r1c, 0, [[1, 128], [0, 128]]))
            V.tensor_copy(stat[1][:], bass.AP(r2c, 0, [[1, 128], [0, 128]]))
            V.tensor_scalar(stat[2][:], stat[0][:], 1.0 / DM, None, OP.mult)
            V.tensor_mul(stat[3][:], stat[2][:], stat[2][:])
            vm = sV.inc(V.scalar_tensor_tensor(stat[4][:], stat[1][:], 1.0 / DM,
                                               stat[3][:], OP.mult, OP.subtract))
            sV.wait(SE, vm)
            SE.activation(stat[5][:], stat[4][:], AF.Ln, bias=cepsb[:])
            sm = sS.inc(SE.activation(stat[5][:], stat[5][:], AF.Exp, scale=-0.5))
            sS.wait(V, sm)
            sV.inc(V.tensor_mul(stat[3][:], stat[2][:], stat[5][:]))
            if tt >= 1:
                sV.wait(Y, xm[tt - 1])
            zl = sY.inc(Y.dma_start(out=zfull[:], in_=zsrc(tt)), 16)
            sY.wait(V, zl)
            if tt >= 1:
                sO.wait(V, x1st[tt - 1])
                if x1T_build:
                    sT.wait(V, x1trm[(tt - 1) * NDM + NDM - 1])
            xm[tt] = sV.inc(V.tensor_scalar(x1t[:], zfull[:],
                                            stat[5][:, 0:1], stat[3][:, 0:1],
                                            OP.mult, OP.subtract))
            sV.wait(Y, xm[tt])
            x1st[tt] = sO.inc(Y.dma_start(out=out_dst(tt), in_=x1t[:]), 16)
            if dbg_pair is not None and DBG_BUILD[0]:
                sO.inc(Y.dma_start(out=dbg_pair[0][tt * 128:(tt + 1) * 128, :],
                                   in_=zfull[:]), 16)
                x1st[tt] = sO.inc(Y.dma_start(out=dbg_pair[1][tt * 128:(tt + 1) * 128, :],
                                              in_=x1t[:]), 16)
            if x1T_build:
                for ic in range(NDM):
                    i = tr_i
                    tr_i += 1
                    sV.wait(TE, xm[tt])
                    if i >= 2:
                        sV.wait(TE, x1evm[i - 2])
                    x1trm[i] = sT.inc(TE.transpose(
                        pbank[6 + i % 2][:, 0:128],
                        x1t[:, ic * 128:(ic + 1) * 128], identf[:]))
                    sT.wait(V, x1trm[i])
                    x1evm[i] = sV.inc(V.tensor_copy(
                        x1T[:, ic * TC + tt * 128:ic * TC + (tt + 1) * 128],
                        pbank[6 + i % 2][:, 0:128]))
        if x1T_build:
            marks["x1Tv"] = sV.n
        marks["x1sO" if x1T_build else "outsO"] = sO.n
        sO.wait(Y)

    def phase3a(kind, eng):
        mm_part(
            kind, eng,
            rhs_src=lambda dw, fc: wod[fc * 128:(fc + 1) * 128, dw * 512:(dw + 1) * 512],
            lhs_sb=ctxT,
            bias_sb=bob_sb,
            res_src=lambda dw, tt: xck[tt * 128:(tt + 1) * 128, dw * 512:(dw + 1) * 512],
            zdst=lambda dw, tt: z1scr.ap()[tt * 128:(tt + 1) * 128,
                                           dw * 512:(dw + 1) * 512],
            lhs_wait="ctxT",
            agw_wait=False,
            markpfx="p3")

    run_phase(phase3a)
    stC.close()
    stX = ExitStack()
    x1T = sball(stX, "x1T", [128, NDM * TC], BF16)

    def phase3b(kind, eng):
        ln_part(kind, eng,
                zsrc=lambda tt: z1scr.ap()[tt * 128:(tt + 1) * 128, :],
                out_dst=lambda tt: x1scr.ap()[tt * 128:(tt + 1) * 128, :],
                dbg_pair=(dbg_t["z1"], dbg_t["x1"]) if DBG_BUILD[0] else None,
                x1T_build=True,
                markpfx="p3")

    def phase4a(kind, eng):
        mm_part(
            kind, eng,
            rhs_src=lambda dw, fc: agw_out.ap()[dw * DM + fc * 128:dw * DM + (fc + 1) * 128, :],
            lhs_sb=x1T,
            bias_sb=fbb_sb,
            res_src=lambda dw, tt: x1scr.ap()[tt * 128:(tt + 1) * 128,
                                              dw * 512:(dw + 1) * 512],
            zdst=lambda dw, tt: z2scr.ap()[tt * 128:(tt + 1) * 128,
                                           dw * 512:(dw + 1) * 512],
            lhs_wait=None,
            agw_wait=True,
            markpfx="p4")

    def phase4b(kind, eng):
        ln_part(kind, eng,
                zsrc=lambda tt: z2scr.ap()[tt * 128:(tt + 1) * 128, :],
                out_dst=lambda tt: out[tt * 128:(tt + 1) * 128, :],
                dbg_pair=None,
                x1T_build=False,
                markpfx="p4")

    run_phase(phase3b)
    run_phase(phase4a)
    run_phase(phase4b)
    stX.close()
    stW.close()
    st.close()
    return nc


_host_consts = {}


def _consts():
    if _host_consts:
        return _host_consts
    half = 64
    inv = 1.0 / (10000.0 ** (np.arange(half, dtype=np.float64) / half))
    ang = np.arange(S, dtype=np.float64)[None, :] * inv[:, None]
    _host_consts["cos"] = np.ascontiguousarray(np.cos(ang).astype(np.float32))
    _host_consts["sin"] = np.ascontiguousarray(np.sin(ang).astype(np.float32))
    cm = np.empty((128, 4 * 512), np.float32)
    for o in range(4):
        keep = (np.arange(512)[None, :] - np.arange(128)[:, None]) >= o * 128
        cm[:, o * 512:(o + 1) * 512] = np.where(keep, np.float32(0.0), np.float32(NEG))
    _host_consts["cm"] = cm
    return _host_consts


_nc_cache = {}


def _get_nc(dbg):
    key = bool(dbg)
    if key not in _nc_cache:
        DBG_BUILD[0] = key
        _nc_cache[key] = build_nc(dbg=key)
    return _nc_cache[key]


def make_in_maps(inputs):
    cst = _consts()
    x = np.ascontiguousarray(np.asarray(inputs["L0_attn_hidden_states"]).reshape(T, DM))
    Wq = np.asarray(inputs["L0_attn_Wq"])
    bq = np.asarray(inputs["L0_attn_bq"])
    Wk = np.asarray(inputs["L0_attn_Wk"])
    bk = np.asarray(inputs["L0_attn_bk"])
    Wv = np.asarray(inputs["L0_attn_Wv"])
    bv = np.asarray(inputs["L0_attn_bv"])
    Wo = np.ascontiguousarray(np.asarray(inputs["L0_attn_Wo"]))
    bo = np.asarray(inputs["L0_attn_bo"])
    pkf = np.asarray(inputs["L0_ffn_weight_packed"]).reshape(DM, 2048)
    scf = np.asarray(inputs["L0_ffn_weight_scales"]).reshape(DM, 32)
    fbv = np.asarray(inputs["L0_ffn_bias"])
    in_maps = []
    for c in range(8):
        in_maps.append({
            "xck": np.ascontiguousarray(x[c * TC:(c + 1) * TC]),
            "wqd": np.ascontiguousarray(Wq[:, c * QH:(c + 1) * QH]),
            "bqv": np.ascontiguousarray(bq[c * QH:(c + 1) * QH].reshape(R, 128).T),
            "wkd": np.ascontiguousarray(Wk[:, c * D:(c + 1) * D]),
            "bkv": np.ascontiguousarray(bk[c * D:(c + 1) * D].reshape(1, 128).T),
            "wvd": np.ascontiguousarray(Wv[:, c * D:(c + 1) * D]),
            "bvv": np.ascontiguousarray(bv[c * D:(c + 1) * D].reshape(1, 128).T),
            "wod": Wo,
            "bod": np.ascontiguousarray(bo.reshape(1, DM)),
            "pkd": np.ascontiguousarray(pkf[c * 512:(c + 1) * 512]),
            "scd": np.ascontiguousarray(scf[c * 512:(c + 1) * 512]),
            "fbd": np.ascontiguousarray(fbv.reshape(1, DM)),
            "cosd": cst["cos"],
            "sind": cst["sin"],
            "cmd": cst["cm"],
        })
    return in_maps


def kernel(**inputs):
    nc = _get_nc(False)
    in_maps = make_in_maps(inputs)
    res = run_bass_kernel_spmd(nc, in_maps, core_ids=list(range(8)))
    o = np.concatenate([res.results[c]["out"] for c in range(8)], axis=0)
    return np.ascontiguousarray(o.reshape(B, S, DM))


# version 06
# speedup vs baseline: 1.1759x; 1.0503x over previous
import numpy as np
import ml_dtypes
from contextlib import ExitStack

import concourse.bass as bass
import concourse.mybir as mybir
from concourse.bass_utils import run_bass_kernel_spmd
from concourse.aot_env import allow_ici_all_to_all

F32 = mybir.dt.float32
BF16 = mybir.dt.bfloat16
I32 = mybir.dt.int32
AF = mybir.ActivationFunctionType
OP = mybir.AluOpType

B, S, DM = 2, 4096, 4096
T = B * S
TC = T // 8
H, G, D, R = 32, 8, 128, 4
QH = R * D
SCALE = 1.0 / float(np.sqrt(np.float64(128.0)))
EXPB = -20.0
NEG = -3.0e38
EPS = 1e-5
NT = TC // 128
NDM = DM // 128
NG = S // 512

DBG_BUILD = [False]


class Px:
    def __init__(self):
        self.eng = None

    def __getattr__(self, name):
        if self.eng is not None:
            return getattr(self.eng, name)
        return lambda *a, **kw: None


class Sem:
    def __init__(self, h):
        self.h = h
        self.n = 0

    def inc(self, inst, k=1):
        if inst is not None:
            inst.then_inc(self.h, k)
        self.n += k
        return self.n

    def wait(self, px, val=None):
        if px.eng is not None:
            px.eng.wait_ge(self.h, self.n if val is None else val)


def build_nc(dbg=False):
    nc = bass.Bass("TRN2", target_bir_lowering=False, debug=False)

    xck = nc.dram_tensor("xck", [TC, DM], F32, kind="ExternalInput").ap()
    wqd = nc.dram_tensor("wqd", [DM, QH], F32, kind="ExternalInput").ap()
    bqv = nc.dram_tensor("bqv", [128, R], F32, kind="ExternalInput").ap()
    wkd = nc.dram_tensor("wkd", [DM, D], F32, kind="ExternalInput").ap()
    bkv = nc.dram_tensor("bkv", [128, 1], F32, kind="ExternalInput").ap()
    wvd = nc.dram_tensor("wvd", [DM, D], F32, kind="ExternalInput").ap()
    bvv = nc.dram_tensor("bvv", [128, 1], F32, kind="ExternalInput").ap()
    wod = nc.dram_tensor("wod", [DM, DM], F32, kind="ExternalInput").ap()
    bod = nc.dram_tensor("bod", [1, DM], F32, kind="ExternalInput").ap()
    pkd = nc.dram_tensor("pkd", [512, 2048], I32, kind="ExternalInput").ap()
    scd = nc.dram_tensor("scd", [512, 32], F32, kind="ExternalInput").ap()
    fbd = nc.dram_tensor("fbd", [1, DM], F32, kind="ExternalInput").ap()
    cosd = nc.dram_tensor("cosd", [64, S], F32, kind="ExternalInput").ap()
    sind = nc.dram_tensor("sind", [64, S], F32, kind="ExternalInput").ap()
    cmd = nc.dram_tensor("cmd", [128, 4 * 512], F32, kind="ExternalInput").ap()
    out = nc.dram_tensor("out", [TC, DM], F32, kind="ExternalOutput").ap()

    agx_in = nc.dram_tensor("agx_in", [DM, TC], BF16)
    agx_out = nc.dram_tensor("agx_out", [8 * DM, TC], BF16)
    agw_in = nc.dram_tensor("agw_in", [DM, 512], BF16)
    agw_out = nc.dram_tensor("agw_out", [8 * DM, 512], BF16)
    a2a_in = nc.dram_tensor("a2a_in", [8, QH, TC], BF16)
    a2a_out = nc.dram_tensor("a2a_out", [8, QH, TC], BF16)
    z1scr = nc.dram_tensor("z1scr", [TC, DM], F32)
    x1scr = nc.dram_tensor("x1scr", [TC, DM], F32)
    z2scr = nc.dram_tensor("z2scr", [TC, DM], F32)

    dbg_t = {}
    if dbg:
        dbg_t["qT"] = nc.dram_tensor("d_qT", [128, R * S], F32, kind="ExternalOutput").ap()
        dbg_t["kT"] = nc.dram_tensor("d_kT", [128, T], F32, kind="ExternalOutput").ap()
        dbg_t["vb"] = nc.dram_tensor("d_vb", [128, T], F32, kind="ExternalOutput").ap()
        dbg_t["ctx"] = nc.dram_tensor("d_ctx", [8 * QH, TC], F32, kind="ExternalOutput").ap()
        dbg_t["z1"] = nc.dram_tensor("d_z1", [TC, DM], F32, kind="ExternalOutput").ap()
        dbg_t["x1"] = nc.dram_tensor("d_x1", [TC, DM], F32, kind="ExternalOutput").ap()
        dbg_t["wt"] = nc.dram_tensor("d_wt", [DM, 512], F32, kind="ExternalOutput").ap()
        dbg_t["win"] = nc.dram_tensor("d_win", [DM, 512], F32, kind="ExternalOutput").ap()
        dbg_t["wtile0"] = nc.dram_tensor("d_wtile0", [128, DM], F32, kind="ExternalOutput").ap()
        dbg_t["wtile1"] = nc.dram_tensor("d_wtile1", [128, DM], F32, kind="ExternalOutput").ap()

    st = ExitStack()

    def sball(stk, nm, sh, dt):
        return stk.enter_context(nc.sbuf_tensor(nm, sh, dt))

    sY = Sem(st.enter_context(nc.semaphore("sY")))
    sG = Sem(st.enter_context(nc.semaphore("sG")))
    sD = Sem(st.enter_context(nc.semaphore("sD")))
    sT = Sem(st.enter_context(nc.semaphore("sT")))
    sV = Sem(st.enter_context(nc.semaphore("sV")))
    sS = Sem(st.enter_context(nc.semaphore("sS")))
    sC = Sem(st.enter_context(nc.semaphore("sC")))
    sO = Sem(st.enter_context(nc.semaphore("sO")))
    s_init = Sem(st.enter_context(nc.semaphore("s_init")))
    s_bias = Sem(st.enter_context(nc.semaphore("s_bias")))
    s_cm = Sem(st.enter_context(nc.semaphore("s_cm")))
    s_tab = Sem(st.enter_context(nc.semaphore("s_tab")))
    s_ct = Sem(st.enter_context(nc.semaphore("s_ct")))
    s_st = [Sem(st.enter_context(nc.semaphore("s_st%d" % i))) for i in range(4)]
    s_xf = [Sem(st.enter_context(nc.semaphore("s_xf%d" % i))) for i in range(2)]
    s_xg = [Sem(st.enter_context(nc.semaphore("s_xg%d" % i))) for i in range(4)]
    s_qd = [Sem(st.enter_context(nc.semaphore("s_qd%d" % i))) for i in range(2)]
    s_o2 = [Sem(st.enter_context(nc.semaphore("s_o2%d" % i))) for i in range(2)]
    s_wo = [Sem(st.enter_context(nc.semaphore("s_wo%d" % i))) for i in range(2)]
    s_xr = [Sem(st.enter_context(nc.semaphore("s_xr%d" % i))) for i in range(2)]
    s_zs = [Sem(st.enter_context(nc.semaphore("s_zs%d" % i))) for i in range(2)]
    all_sems = ([sY, sG, sD, sT, sV, sS, sC, sO, s_init, s_bias, s_cm, s_tab, s_ct]
                + s_st + s_xf + s_xg + s_qd + s_o2 + s_wo + s_xr + s_zs)

    pall = st.enter_context(nc.psum_tensor("pall", [128, 4096], F32))
    pbank = [pall.ap()[:, i * 512:(i + 1) * 512] for i in range(8)]
    pallb = pall.ap().bitcast(BF16)

    identf = sball(st, "identf", [128, 128], F32)
    io_a = sball(st, "io_a", [128, 128], I32)
    io_b = sball(st, "io_b", [128, 128], I32)
    onecol = sball(st, "onecol", [128, 1], BF16)
    onerow = sball(st, "onerow", [1, 128], F32)
    cexpb = sball(st, "cexpb", [128, 1], F32)
    cepsb = sball(st, "cepsb", [128, 1], F32)
    bq_sb = sball(st, "bq_sb", [128, R], F32)
    bk_sb = sball(st, "bk_sb", [128, 1], F32)
    bv_sb = sball(st, "bv_sb", [128, 1], F32)

    def run_phase(fn):
        snap = [s_.n for s_ in all_sems]

        def mk(kind):
            def f(eng):
                for s_, v in zip(all_sems, snap):
                    s_.n = v
                fn(kind, eng)
            return f

        blk = nc.Block()
        blk.__enter__()
        blk.gpsimd(mk('g'))
        blk.vector(mk('v'))
        blk.scalar(mk('s'))
        blk.tensor(mk('t'))
        blk.sync(mk('y'))
        blk.__exit__(None, None, None)

    marks = {}

    ph0 = ExitStack()
    xf32 = [sball(ph0, "xf%d" % i, [128, DM], F32) for i in range(2)]
    xb8 = sball(ph0, "xb8", [128, NT * DM], BF16)
    identb = sball(ph0, "identb", [128, 128], BF16)
    xTrow = [sball(ph0, "xTr%d" % i, [128, TC], BF16) for i in range(2)]
    pkt = sball(ph0, "pkt", [128, 2048], I32)
    sct = sball(ph0, "sct", [128, 32], F32)
    d_n = sball(ph0, "d_n", [128, 1024], I32)
    d_s = sball(ph0, "d_s", [128, 1024], I32)
    d_i = sball(ph0, "d_i", [128, 1024], I32)
    d_e = sball(ph0, "d_e", [128, 1024], I32)
    d_m = sball(ph0, "d_m", [128, 1024], I32)
    d_a = sball(ph0, "d_a", [128, 1024], F32)
    d_u = sball(ph0, "d_u", [128, 1024], F32)
    d_q = sball(ph0, "d_q", [128, 1024], F32)
    d_t = sball(ph0, "d_t", [128, 1024], F32)
    d_g = sball(ph0, "d_g", [128, 1024], F32)
    wtile = sball(ph0, "wtile", [128, DM], F32)
    wall = sball(ph0, "wall", [128, 4 * DM], BF16)
    wTrow = [sball(ph0, "wTw%d" % i, [128, 512], BF16) for i in range(2)]

    def phase0(kind, eng):
        PXs = {k: Px() for k in "gvsty"}
        PXs[kind].eng = eng
        GE, V, SE, TE, Y = PXs['g'], PXs['v'], PXs['s'], PXs['t'], PXs['y']

        GE.memset(onecol[:], 1.0)
        GE.memset(onerow[:], 1.0)
        GE.memset(cexpb[:], EXPB)
        GE.memset(cepsb[:], EPS)
        GE.iota(io_a[:], [[1, 128]], base=0, channel_multiplier=0,
                allow_small_or_imprecise_dtypes=True)
        s_init.inc(GE.iota(io_b[:], [[0, 128]], base=0, channel_multiplier=1,
                           allow_small_or_imprecise_dtypes=True))
        s_bias.inc(GE.dma_start(out=bq_sb[:], in_=bqv), 16)
        s_bias.inc(GE.dma_start(out=bk_sb[:], in_=bkv), 16)
        s_bias.inc(GE.dma_start(out=bv_sb[:], in_=bvv), 16)
        s_init.wait(V)
        idm = sV.inc(V.tensor_tensor(identf[:], io_a[:], io_b[:], OP.is_equal))
        idb = sV.inc(V.tensor_copy(identb[:], identf[:]))
        sV.wait(TE, idb)

        cast_m = {}
        for t in range(NT):
            ldx = s_xf[t % 2].inc(Y.dma_start(out=xf32[t % 2][:],
                                              in_=xck[t * 128:(t + 1) * 128, :]), 16)
            s_xf[t % 2].wait(V, ldx)
            cast_m[t] = sV.inc(V.tensor_copy(xb8[:, t * DM:(t + 1) * DM], xf32[t % 2][:]))
            if t + 2 < NT:
                sV.wait(Y, cast_m[t])
        pbb = [pallb[:, (4 + i) * 1024:(4 + i) * 1024 + 128] for i in range(4)]
        trm = {}
        evm = {}
        stm = {}
        for db in range(NDM):
            for t in range(NT):
                i = db * NT + t
                sV.wait(TE, cast_m[t])
                if i >= 4:
                    sV.wait(TE, evm[i - 4])
                trm[i] = sT.inc(TE.transpose(
                    pbb[i % 4], xb8[:, t * DM + db * 128:t * DM + (db + 1) * 128],
                    identb[:]))
                sT.wait(V, trm[i])
                if db >= 2 and t == 0:
                    s_st[db % 2].wait(V, stm[db - 2])
                evm[i] = sV.inc(V.tensor_copy(
                    xTrow[db % 2][:, t * 128:(t + 1) * 128], pbb[i % 4]))
            sV.wait(Y, evm[db * NT + NT - 1])
            stm[db] = s_st[db % 2].inc(Y.dma_start(
                out=agx_in.ap()[db * 128:(db + 1) * 128, :], in_=xTrow[db % 2][:]), 16)
        for s_ in s_st[0:2]:
            s_.wait(GE)
        marks["agx"] = sC.inc(GE.collective_compute(
            "AllGather", OP.bypass, replica_groups=[list(range(8))],
            ins=[agx_in.ap().opt()], outs=[agx_out.ap().opt()]))

        vdeq = {}
        for ot in range(4):
            if ot >= 1:
                sV.wait(GE, vdeq[ot - 1])
            lp = sG.inc(GE.dma_start(out=pkt[:], in_=pkd[ot * 128:(ot + 1) * 128, :]), 16)
            lp = sG.inc(GE.dma_start(out=sct[:], in_=scd[ot * 128:(ot + 1) * 128, :]), 16)
            sG.wait(V, lp)
            for half in range(2):
                co = half * 1024
                pslice = pkt[:, co:co + 1024]
                for hi in range(2):
                    if hi == 0:
                        V.tensor_single_scalar(d_n[:], pslice, 4, OP.logical_shift_right)
                    else:
                        V.tensor_single_scalar(d_n[:], pslice, 15, OP.bitwise_and)
                    V.tensor_single_scalar(d_s[:], d_n[:], 3, OP.logical_shift_right)
                    V.tensor_single_scalar(d_i[:], d_n[:], 7, OP.bitwise_and)
                    V.tensor_single_scalar(d_e[:], d_i[:], 1, OP.logical_shift_right)
                    V.tensor_single_scalar(d_m[:], d_i[:], 1, OP.bitwise_and)
                    V.tensor_scalar(d_a[:], d_e[:], 2.0, 2.0, OP.mult, OP.min)
                    V.tensor_scalar(d_u[:], d_e[:], 0.5, 0.5, OP.mult, OP.max)
                    V.tensor_scalar_add(d_u[:], d_u[:], -0.25)
                    V.tensor_mul(d_q[:], d_u[:], d_u[:])
                    V.scalar_tensor_tensor(d_t[:], d_m[:], 1.0, d_a[:], OP.mult, OP.add)
                    V.scalar_tensor_tensor(d_g[:], d_q[:], 0.4375, d_t[:], OP.add, OP.mult)
                    V.tensor_scalar(d_t[:], d_s[:], -2.0, 1.0, OP.mult, OP.add)
                    V.tensor_mul(d_g[:], d_g[:], d_t[:])
                    ssrc = bass.AP(sct, half * 16, [[32, 128], [1, 16], [0, 64]])
                    dsts = bass.AP(wtile, 2 * co + hi, [[DM, 128], [2, 1024]])
                    V.tensor_tensor(dsts, d_g[:], ssrc, OP.mult)
            vdeq[ot] = sV.inc(V.tensor_copy(wall[:, ot * DM:(ot + 1) * DM], wtile[:]))
        wtrm = {}
        wevm = {}
        wstm = {}
        for ib in range(NDM):
            for ot in range(4):
                i = ib * 4 + ot
                sV.wait(TE, vdeq[ot])
                if i >= 4:
                    sV.wait(TE, wevm[i - 4])
                wtrm[i] = sT.inc(TE.transpose(
                    pbb[i % 4], wall[:, ot * DM + ib * 128:ot * DM + (ib + 1) * 128],
                    identb[:]))
                sT.wait(V, wtrm[i])
                if ib >= 2 and ot == 0:
                    s_st[2 + ib % 2].wait(V, wstm[ib - 2])
                wevm[i] = sV.inc(V.tensor_copy(
                    wTrow[ib % 2][:, ot * 128:(ot + 1) * 128], pbb[i % 4]))
            sV.wait(Y, wevm[ib * 4 + 3])
            wstm[ib] = s_st[2 + ib % 2].inc(Y.dma_start(
                out=agw_in.ap()[ib * 128:(ib + 1) * 128, :], in_=wTrow[ib % 2][:]), 16)
        for s_ in s_st[2:4]:
            s_.wait(GE)
        sC.wait(GE, marks["agx"])
        marks["agw"] = sC.inc(GE.collective_compute(
            "AllGather", OP.bypass, replica_groups=[list(range(8))],
            ins=[agw_in.ap().opt()], outs=[agw_out.ap().opt()]))

    run_phase(phase0)
    ph0.close()

    stA = ExitStack()
    wq_sb = sball(stA, "wq_sb", [128, NDM * QH], BF16)
    wk_sb = sball(stA, "wk_sb", [128, NDM * D], BF16)
    wv_sb = sball(stA, "wv_sb", [128, NDM * D], BF16)
    csl = [sball(stA, "csl0", [64, 512], F32)]
    snl = [sball(stA, "snl0", [64, 512], F32)]
    cm_sb = sball(stA, "cm_sb", [128, 4 * 512], BF16)
    qT_sb = sball(stA, "qT_sb", [128, R * S], BF16)
    kT_sb = sball(stA, "kT_sb", [128, T], BF16)
    vb_sb = sball(stA, "vb_sb", [128, T], BF16)
    xg = [sball(stA, "xg%d" % i, [128, 8, 512], BF16) for i in range(4)]
    qa = [sball(stA, "qa%d" % i, [128, 512], F32) for i in range(2)]
    qb = [sball(stA, "qb%d" % i, [64, 512], F32) for i in range(2)]
    rp = [sball(stA, "rp%d" % i, [64, 512], F32) for i in range(2)]
    o2s = [sball(stA, "o2s%d" % i, [64, 512], BF16) for i in range(2)]
    vst = sball(stA, "vst", [128, 512], F32)
    prT = sball(stA, "prT", [128, 32 * 512], BF16)
    dstg = [sball(stA, "dstg0", [128, 512], F32)]
    ctxe = [sball(stA, "ctxe0", [128, 512], BF16)]

    def phase1(b):
        def fn(kind, eng):
            PXs = {k: Px() for k in "gvsty"}
            PXs[kind].eng = eng
            GE, V, SE, TE, Y = PXs['g'], PXs['v'], PXs['s'], PXs['t'], PXs['y']

            if b == 0:
                for dmc in range(NDM):
                    sG.inc(GE.dma_start(out=wq_sb[:, dmc * QH:(dmc + 1) * QH],
                                        in_=wqd[dmc * 128:(dmc + 1) * 128, :]), 16)
                    sG.inc(GE.dma_start(out=wk_sb[:, dmc * D:(dmc + 1) * D],
                                        in_=wkd[dmc * 128:(dmc + 1) * 128, :]), 16)
                    sG.inc(GE.dma_start(out=wv_sb[:, dmc * D:(dmc + 1) * D],
                                        in_=wvd[dmc * 128:(dmc + 1) * 128, :]), 16)
                s_cm.inc(GE.dma_start(out=cm_sb[:], in_=cmd), 16)
                marks["wqkv"] = sG.n
                marks["cmld"] = s_cm.n

            mmhalf = {}
            ldm = {}
            evfb = {}
            ropev = {}
            roped = {}
            qadma = {}
            vtr = {}
            vev = {}
            for tg in range(8):
                gtok = b * S + tg * 512
                rank = gtok // TC
                coff = gtok % TC
                cl = tg * 512
                if tg >= 1:
                    sV.wait(Y, ropev[(tg - 1) * 5 + 4])
                s_tab.inc(Y.dma_start(out=csl[0][:], in_=cosd[:, cl:cl + 512]), 16)
                tabm = s_tab.inc(Y.dma_start(out=snl[0][:], in_=sind[:, cl:cl + 512]), 16)
                for q4 in range(4):
                    if tg > 0:
                        sT.wait(Y, mmhalf[(tg - 1, q4)])
                    lm = 0
                    sC.wait(Y, marks["agx"])
                    for j in range(8):
                        dmc = q4 * 8 + j
                        lm = s_xg[q4].inc(Y.dma_start(
                            out=xg[q4][:, j, :],
                            in_=agx_out.ap()[rank * DM + dmc * 128:rank * DM + dmc * 128 + 128,
                                             coff:coff + 512]), 16)
                    ldm[(tg, q4)] = lm
                sG.wait(TE, marks["wqkv"])
                for q4 in range(4):
                    s_xg[q4].wait(TE, ldm[(tg, q4)])
                    for j in range(8):
                        dmc = q4 * 8 + j
                        first = (dmc == 0)
                        last = (dmc == NDM - 1)
                        if first and tg > 0:
                            for bkk in range(6):
                                sV.wait(TE, evfb[(tg - 1, bkk)])
                        TE.matmul(pbank[0][:, 0:512],
                                  wq_sb[:, dmc * QH + 0 * 128:dmc * QH + 1 * 128],
                                  xg[q4][:, j, :], start=first, stop=last)
                        TE.matmul(pbank[1][:, 0:512],
                                  wq_sb[:, dmc * QH + 1 * 128:dmc * QH + 2 * 128],
                                  xg[q4][:, j, :], start=first, stop=last)
                        TE.matmul(pbank[2][:, 0:512],
                                  wq_sb[:, dmc * QH + 2 * 128:dmc * QH + 3 * 128],
                                  xg[q4][:, j, :], start=first, stop=last)
                        TE.matmul(pbank[3][:, 0:512],
                                  wq_sb[:, dmc * QH + 3 * 128:dmc * QH + 4 * 128],
                                  xg[q4][:, j, :], start=first, stop=last)
                        TE.matmul(pbank[4][:, 0:512],
                                  wk_sb[:, dmc * D:(dmc + 1) * D],
                                  xg[q4][:, j, :], start=first, stop=last)
                        inst = TE.matmul(pbank[5][:, 0:512],
                                         wv_sb[:, dmc * D:(dmc + 1) * D],
                                         xg[q4][:, j, :], start=first, stop=last)
                        if j == 7:
                            mmhalf[(tg, q4)] = sT.inc(inst)
                mmend = mmhalf[(tg, 3)]
                sT.wait(V, mmend)
                if b == 0 and tg == 0:
                    s_bias.wait(V, 48)
                s_tab.wait(V, tabm)
                cs = csl[0][:]
                sn = snl[0][:]
                for fb in range(5):
                    qi = fb % 2
                    tix = tg * 5 + fb
                    bias = bq_sb[:, fb:fb + 1] if fb < 4 else bk_sb[:, 0:1]
                    prev_qi = tix - {0: 1, 1: 3, 2: 2, 3: 2, 4: 2}[fb]
                    if prev_qi >= 0:
                        s_qd[qi].wait(V, qadma.get(prev_qi, 0) or 0)
                    evfb[(tg, fb)] = sV.inc(V.tensor_scalar(
                        qa[qi][:], pbank[fb][:, 0:512], bias, None, OP.add))
                    sV.wait(SE, evfb[(tg, fb)])
                    if prev_qi >= 0:
                        sV.wait(SE, ropev.get(prev_qi, 0) or 0)
                    qadma[tix] = s_qd[qi].inc(SE.dma_start(out=qb[qi][:], in_=qa[qi][64:128, :]), 16)
                    s_qd[qi].wait(V, qadma[tix])
                    dst_lo = (qT_sb[0:64, fb * S + cl:fb * S + cl + 512] if fb < 4
                              else kT_sb[0:64, gtok:gtok + 512])
                    V.tensor_tensor(rp[0][:], qa[qi][0:64, :], cs, OP.mult)
                    V.tensor_tensor(rp[1][:], qb[qi][:], sn, OP.mult)
                    V.tensor_sub(dst_lo, rp[0][:], rp[1][:])
                    V.tensor_tensor(rp[0][:], qa[qi][0:64, :], sn, OP.mult)
                    V.tensor_tensor(rp[1][:], qb[qi][:], cs, OP.mult)
                    if tix >= 2:
                        s_o2[tix % 2].wait(V, roped.get(tix - 2, 0) or 0)
                    ropev[tix] = sV.inc(V.tensor_add(o2s[tix % 2][:], rp[0][:], rp[1][:]))
                    sV.wait(SE, ropev[tix])
                    if fb < 4:
                        dst_hi = qT_sb[64:128, fb * S + cl:fb * S + cl + 512]
                    else:
                        dst_hi = kT_sb[64:128, gtok:gtok + 512]
                    roped[tix] = s_o2[tix % 2].inc(SE.dma_start(out=dst_hi, in_=o2s[tix % 2][:]), 16)
                if tg > 0:
                    sT.wait(V, vtr[(tg - 1) * 4 + 3])
                evfb[(tg, 5)] = sV.inc(V.tensor_scalar(
                    vst[:], pbank[5][:, 0:512], bv_sb[:, 0:1], None, OP.add))
                for j in range(4):
                    i = tg * 4 + j
                    sV.wait(TE, evfb[(tg, 5)])
                    if i >= 2:
                        sV.wait(TE, vev[i - 2])
                    vtr[i] = sT.inc(TE.transpose(pbank[6 + i % 2][:, 0:128],
                                                 vst[:, j * 128:(j + 1) * 128], identf[:]))
                    sT.wait(V, vtr[i])
                    vev[i] = sV.inc(V.tensor_copy(
                        vb_sb[:, gtok + j * 128:gtok + (j + 1) * 128],
                        pbank[6 + i % 2][:, 0:128]))
            marks["ropeD%d" % b] = (s_o2[0].n, s_o2[1].n, s_qd[0].n, s_qd[1].n)
        return fn

    def phase2(b):
        def fn(kind, eng):
            PXs = {k: Px() for k in "gvsty"}
            PXs[kind].eng = eng
            GE, V, SE, TE, Y = PXs['g'], PXs['v'], PXs['s'], PXs['t'], PXs['y']
            boff = b * S
            m0, m1, m2, m3 = marks["ropeD%d" % b]
            s_o2[0].wait(TE, m0)
            s_o2[1].wait(TE, m1)
            s_qd[0].wait(TE, m2)
            s_qd[1].wait(TE, m3)
            s_cm.wait(V, marks["cmld"])

            pafree = {0: None, 1: None}
            prread = {}
            dnfree = {0: None}
            bc6free = {0: None}
            cxfree = {0: None}
            ctfree = {0: None}
            dgfree = {0: None}
            dgbank = {6: None, 7: None}

            for grp in range(NG):
                nkb = 4 * (grp + 1)
                nfull = nkb - 4
                gq = boff + grp * 512
                dest = gq // TC
                qoff = gq % TC
                units = ([("pair", p) for p in range(nfull // 2)]
                         + [("diag", d) for d in range(4)])
                for h in range(4):
                    expm = {}
                    prev_kbs = None

                    def dnpv(kbs):
                        nonlocal_marker = None
                        for kd in kbs:
                            sS.wait(TE, expm[kd])
                            if kd == 0:
                                if dnfree[0] is not None:
                                    sV.wait(TE, dnfree[0])
                                if cxfree[0] is not None:
                                    sV.wait(TE, cxfree[0])
                            TE.matmul(pbank[5][0:1, 0:512], onecol[:],
                                      prT[:, kd * 512:(kd + 1) * 512],
                                      start=(kd == 0), stop=(kd == nkb - 1))
                            pv_in = TE.matmul(pbank[4][:, 0:512],
                                              vb_sb[:, boff + kd * 128:boff + (kd + 1) * 128],
                                              prT[:, kd * 512:(kd + 1) * 512],
                                              start=(kd == 0), stop=(kd == nkb - 1))
                            prread[kd] = sT.inc(pv_in)

                    for u, (kind_u, idx_u) in enumerate(units):
                        if kind_u == "pair":
                            kb0 = 2 * idx_u
                            sl = idx_u % 2
                            if pafree[sl] is not None:
                                sS.wait(TE, pafree[sl])
                            qrhs = qT_sb[:, h * S + grp * 512:h * S + (grp + 1) * 512]
                            TE.matmul(pall.ap()[:, sl * 1024:sl * 1024 + 512],
                                      kT_sb[:, boff + kb0 * 128:boff + (kb0 + 1) * 128],
                                      qrhs, start=True, stop=True)
                            mmm = sT.inc(TE.matmul(
                                pall.ap()[:, sl * 1024 + 512:sl * 1024 + 1024],
                                kT_sb[:, boff + (kb0 + 1) * 128:boff + (kb0 + 2) * 128],
                                qrhs, start=True, stop=True))
                            if prev_kbs is not None:
                                dnpv(prev_kbs)
                            sT.wait(SE, mmm)
                            if prread.get(kb0 + 1) is not None:
                                sT.wait(SE, prread[kb0 + 1])
                            if prread.get(kb0) is not None:
                                sT.wait(SE, prread[kb0])
                            ex = sS.inc(SE.activation(
                                prT[:, kb0 * 512:(kb0 + 2) * 512],
                                pall.ap()[:, sl * 1024:(sl + 1) * 1024],
                                AF.Exp, bias=cexpb[:], scale=SCALE))
                            expm[kb0] = expm[kb0 + 1] = ex
                            pafree[sl] = ex
                            prev_kbs = (kb0, kb0 + 1)
                        else:
                            d = idx_u
                            kb = nfull + d
                            oi = d
                            bnk = 6 + d % 2
                            if bnk == 6 and bc6free[0] is not None:
                                sV.wait(TE, bc6free[0])
                            if dgbank[bnk] is not None:
                                sV.wait(TE, dgbank[bnk])
                            mmm = sT.inc(TE.matmul(
                                pbank[bnk][:, 0:512],
                                kT_sb[:, boff + kb * 128:boff + (kb + 1) * 128],
                                qT_sb[:, h * S + grp * 512:h * S + (grp + 1) * 512],
                                start=True, stop=True))
                            if prev_kbs is not None:
                                dnpv(prev_kbs)
                            sT.wait(V, mmm)
                            if dgfree[0] is not None:
                                sS.wait(V, dgfree[0])
                            stmk = sV.inc(V.scalar_tensor_tensor(
                                dstg[0][:], pbank[bnk][:, 0:512], SCALE,
                                cm_sb[:, oi * 512:(oi + 1) * 512], OP.mult, OP.add))
                            dgbank[bnk] = stmk
                            sV.wait(SE, stmk)
                            if prread.get(kb) is not None:
                                sT.wait(SE, prread[kb])
                            ex = sS.inc(SE.activation(
                                prT[:, kb * 512:(kb + 1) * 512], dstg[0][:],
                                AF.Exp, bias=cexpb[:], scale=1.0))
                            expm[kb] = ex
                            dgfree[0] = ex
                            prev_kbs = (kb,)
                    dnpv(prev_kbs)
                    em = sT.n
                    sT.wait(V, em)
                    V.tensor_copy(rp[0][0:1, :], pbank[5][0:1, 0:512])
                    dn2 = sV.inc(V.reciprocal(rp[1][0:1, :], rp[0][0:1, :]))
                    dnfree[0] = dn2
                    sV.wait(TE, dn2)
                    dbm = sT.inc(TE.matmul(pbank[6][:, 0:512], onerow[:],
                                           rp[1][0:1, :], start=True, stop=True))
                    ci = 0
                    sT.wait(V, dbm)
                    bcm = sV.inc(V.tensor_copy(vst[:, 0:512], pbank[6][:, 0:512]))
                    bc6free[0] = bcm
                    if ctfree[ci] is not None:
                        sY.wait(V, ctfree[ci])
                    fm = sV.inc(V.tensor_tensor(ctxe[ci][:], pbank[4][:, 0:512],
                                                vst[:, 0:512], OP.mult))
                    cxfree[0] = fm
                    sV.wait(Y, fm)
                    ctfree[ci] = sY.inc(Y.dma_start(
                        out=a2a_in.ap()[dest, h * 128:(h + 1) * 128, qoff:qoff + 512],
                        in_=ctxe[ci][:]), 16)
        return fn

    run_phase(phase1(0))
    run_phase(phase2(0))
    run_phase(phase1(1))
    run_phase(phase2(1))

    def phase_a2a(kind, eng):
        PXs = {k: Px() for k in "gvsty"}
        PXs[kind].eng = eng
        GE = PXs['g']
        sY.wait(GE)
        with allow_ici_all_to_all():
            marks["a2a"] = sC.inc(GE.collective_compute(
                "AllToAll", OP.bypass, replica_groups=[list(range(8))],
                ins=[a2a_in.ap().opt()], outs=[a2a_out.ap().opt()]))
        if DBG_BUILD[0]:
            GE.wait_ge(sC.h, marks["a2a"])
            sO.inc(GE.dma_start(out=dbg_t["ctx"],
                                in_=a2a_out.ap().rearrange("a b c -> (a b) c")), 16)
            sO.inc(GE.dma_start(out=dbg_t["qT"], in_=qT_sb[:]), 16)
            sO.inc(GE.dma_start(out=dbg_t["kT"], in_=kT_sb[:]), 16)
            sO.inc(GE.dma_start(out=dbg_t["vb"], in_=vb_sb[:]), 16)
            sO.inc(GE.dma_start(out=dbg_t["wt"], in_=agw_out.ap()[0:DM, :]), 16)
            sO.inc(GE.dma_start(out=dbg_t["win"], in_=agw_in.ap()), 16)
            sO.wait(GE)

    run_phase(phase_a2a)
    stA.close()

    stW = ExitStack()
    woh = sball(stW, "woh", [128, NDM, 512], BF16)
    bob_sb = sball(stW, "bob_sb", [128, DM], F32)
    fbb_sb = sball(stW, "fbb_sb", [128, DM], F32)
    zc = [sball(stW, "zc%d" % i, [128, 512], F32) for i in range(2)]
    zd = [sball(stW, "zd%d" % i, [128, 512], F32) for i in range(2)]
    sqd = sball(stW, "sqd", [128, 512], F32)
    xres = [sball(stW, "xres%d" % i, [128, 512], F32) for i in range(2)]
    sums = sball(stW, "sums", [128, NT * 8], F32)
    sqs = sball(stW, "sqs", [128, NT * 8], F32)
    stat = [sball(stW, "stat%d" % i, [128, 128], F32) for i in range(6)]
    r1c = sball(stW, "r1c", [128, 1], F32)
    r2c = sball(stW, "r2c", [128, 1], F32)
    zfull = sball(stW, "zfull", [128, DM], F32)
    x1t = sball(stW, "x1t", [128, DM], F32)

    stC = ExitStack()
    ctxT = sball(stC, "ctxT", [128, NDM * TC], BF16)
    bf_row = sball(stC, "bf_row", [1, DM], F32)

    def mm_part(kind, eng, *, rhs_src, lhs_sb, bias_sb, res_src, zdst,
                lhs_wait, agw_wait, markpfx):
        PXs = {k: Px() for k in "gvsty"}
        PXs[kind].eng = eng
        GE, V, SE, TE, Y = PXs['g'], PXs['v'], PXs['s'], PXs['t'], PXs['y']

        if markpfx == "p3":
            sC.wait(Y, marks["a2a"])
            lm = 0
            a2f = a2a_out.ap().rearrange("a b c -> (a b) c")
            for fc in range(NDM):
                lm = s_ct.inc(Y.dma_start(out=ctxT[:, fc * TC:(fc + 1) * TC],
                                          in_=a2f[fc * 128:(fc + 1) * 128, :]), 16)
            marks["ctxT"] = lm
            for (src_row, dstsb) in ((bod, bob_sb), (fbd, fbb_sb)):
                bfl = s_bias.inc(GE.dma_start(out=bf_row[:], in_=src_row), 16)
                s_bias.wait(TE, bfl)
                mlast = None
                for w in range(8):
                    m1 = sT.inc(TE.matmul(pbank[2][:, 0:512], onerow[:],
                                          bf_row[0:1, w * 512:(w + 1) * 512],
                                          start=True, stop=True))
                    sT.wait(V, m1)
                    m2 = sV.inc(V.tensor_copy(dstsb[:, w * 512:(w + 1) * 512],
                                              pbank[2][:, 0:512]))
                    sV.wait(TE, m2)
                    mlast = m1
                sT.wait(GE, mlast)
            marks["bcb"] = sV.n
        else:
            sO.wait(Y, marks["x1sO"])

        wldm = {}
        mmend = {}
        e2m = {}
        sqm = {}
        stm_ = {}
        xldm = {}
        for dw in range(8):
            for half in range(2):
                if dw > 0:
                    sT.wait(GE, mmend[(dw - 1, half)])
                if agw_wait:
                    sC.wait(GE, marks["agw"])
                lm = 0
                for fc in range(half * 16, half * 16 + 16):
                    lm = s_wo[half].inc(GE.dma_start(out=woh[:, fc, :], in_=rhs_src(dw, fc)), 16)
                wldm[(dw, half)] = lm
            for tt in range(NT):
                bk = tt % 2
                i34 = dw * NT + tt
                if i34 >= 2:
                    sV.wait(Y, e2m[i34 - 2])
                xldm[i34] = s_xr[bk].inc(Y.dma_start(out=xres[bk][:], in_=res_src(dw, tt)), 16)
                if lhs_wait == "ctxT":
                    s_ct.wait(TE, marks["ctxT"])
                if i34 >= 2:
                    sV.wait(TE, e2m[i34 - 2])
                for half in range(2):
                    s_wo[half].wait(TE, wldm[(dw, half)])
                    inst = None
                    for fc in range(half * 16, half * 16 + 16):
                        inst = TE.matmul(pbank[bk][:, 0:512],
                                         lhs_sb[:, fc * TC + tt * 128:fc * TC + (tt + 1) * 128],
                                         woh[:, fc, :],
                                         start=(fc == 0), stop=(fc == NDM - 1))
                    mmend[(dw, half)] = sT.inc(inst)
                m = mmend[(dw, 1)]
                sT.wait(V, m)
                if markpfx == "p3":
                    sV.wait(V, marks["bcb"])
                s_xr[bk].wait(V, xldm[i34])
                if i34 >= 2:
                    sS.wait(V, sqm[i34 - 2])
                    s_zs[bk].wait(V, stm_[i34 - 2])
                V.tensor_tensor(zc[bk][:], pbank[bk][:, 0:512],
                                bias_sb[:, dw * 512:(dw + 1) * 512], OP.add)
                e2m[i34] = sV.inc(V.scalar_tensor_tensor(
                    zd[bk][:], zc[bk][:], 1.0, xres[bk][:], OP.mult, OP.add,
                    accum_out=sums[:, tt * 8 + dw:tt * 8 + dw + 1]))
                sV.wait(SE, e2m[i34])
                sqm[i34] = sS.inc(SE.activation(
                    sqd[:], zd[bk][:], AF.Square,
                    accum_out=sqs[:, tt * 8 + dw:tt * 8 + dw + 1]))
                sS.wait(Y, sqm[i34])
                stm_[i34] = s_zs[bk].inc(Y.dma_start(out=zdst(dw, tt), in_=zd[bk][:]), 16)
        marks[markpfx + "sS"] = sS.n
        marks[markpfx + "sO"] = (s_zs[0].n, s_zs[1].n)

    def ln_part(kind, eng, *, zsrc, out_dst, dbg_pair, x1T_build, markpfx):
        PXs = {k: Px() for k in "gvsty"}
        PXs[kind].eng = eng
        GE, V, SE, TE, Y = PXs['g'], PXs['v'], PXs['s'], PXs['t'], PXs['y']
        xm = {}
        x1trm = {}
        x1evm = {}
        x1st = {}
        tr_i = 0
        z0, z1_ = marks[markpfx + "sO"]
        s_zs[0].wait(Y, z0)
        s_zs[1].wait(Y, z1_)
        for tt in range(NT):
            s_zs[0].wait(V, z0)
            s_zs[1].wait(V, z1_)
            sS.wait(V, marks[markpfx + "sS"])
            V.tensor_reduce(r1c[:], sums[:, tt * 8:tt * 8 + 8], mybir.AxisListType.X, OP.add)
            V.tensor_reduce(r2c[:], sqs[:, tt * 8:tt * 8 + 8], mybir.AxisListType.X, OP.add)
            V.tensor_copy(stat[0][:], bass.AP(r1c, 0, [[1, 128], [0, 128]]))
            V.tensor_copy(stat[1][:], bass.AP(r2c, 0, [[1, 128], [0, 128]]))
            V.tensor_scalar(stat[2][:], stat[0][:], 1.0 / DM, None, OP.mult)
            V.tensor_mul(stat[3][:], stat[2][:], stat[2][:])
            vm = sV.inc(V.scalar_tensor_tensor(stat[4][:], stat[1][:], 1.0 / DM,
                                               stat[3][:], OP.mult, OP.subtract))
            sV.wait(SE, vm)
            SE.activation(stat[5][:], stat[4][:], AF.Ln, bias=cepsb[:])
            sm = sS.inc(SE.activation(stat[5][:], stat[5][:], AF.Exp, scale=-0.5))
            sS.wait(V, sm)
            sV.inc(V.tensor_mul(stat[3][:], stat[2][:], stat[5][:]))
            if tt >= 1:
                sV.wait(Y, xm[tt - 1])
            zl = sY.inc(Y.dma_start(out=zfull[:], in_=zsrc(tt)), 16)
            sY.wait(V, zl)
            if tt >= 1:
                sO.wait(V, x1st[tt - 1])
                if x1T_build:
                    sT.wait(V, x1trm[(tt - 1) * NDM + NDM - 1])
            xm[tt] = sV.inc(V.tensor_scalar(x1t[:], zfull[:],
                                            stat[5][:, 0:1], stat[3][:, 0:1],
                                            OP.mult, OP.subtract))
            sV.wait(Y, xm[tt])
            x1st[tt] = sO.inc(Y.dma_start(out=out_dst(tt), in_=x1t[:]), 16)
            if dbg_pair is not None and DBG_BUILD[0]:
                sO.inc(Y.dma_start(out=dbg_pair[0][tt * 128:(tt + 1) * 128, :],
                                   in_=zfull[:]), 16)
                x1st[tt] = sO.inc(Y.dma_start(out=dbg_pair[1][tt * 128:(tt + 1) * 128, :],
                                              in_=x1t[:]), 16)
            if x1T_build:
                for ic in range(NDM):
                    i = tr_i
                    tr_i += 1
                    sV.wait(TE, xm[tt])
                    if i >= 2:
                        sV.wait(TE, x1evm[i - 2])
                    x1trm[i] = sT.inc(TE.transpose(
                        pbank[6 + i % 2][:, 0:128],
                        x1t[:, ic * 128:(ic + 1) * 128], identf[:]))
                    sT.wait(V, x1trm[i])
                    x1evm[i] = sV.inc(V.tensor_copy(
                        x1T[:, ic * TC + tt * 128:ic * TC + (tt + 1) * 128],
                        pbank[6 + i % 2][:, 0:128]))
        if x1T_build:
            marks["x1Tv"] = sV.n
        marks["x1sO" if x1T_build else "outsO"] = sO.n
        sO.wait(Y)

    def phase3a(kind, eng):
        mm_part(
            kind, eng,
            rhs_src=lambda dw, fc: wod[fc * 128:(fc + 1) * 128, dw * 512:(dw + 1) * 512],
            lhs_sb=ctxT,
            bias_sb=bob_sb,
            res_src=lambda dw, tt: xck[tt * 128:(tt + 1) * 128, dw * 512:(dw + 1) * 512],
            zdst=lambda dw, tt: z1scr.ap()[tt * 128:(tt + 1) * 128,
                                           dw * 512:(dw + 1) * 512],
            lhs_wait="ctxT",
            agw_wait=False,
            markpfx="p3")

    run_phase(phase3a)
    stC.close()
    stX = ExitStack()
    x1T = sball(stX, "x1T", [128, NDM * TC], BF16)

    def phase3b(kind, eng):
        ln_part(kind, eng,
                zsrc=lambda tt: z1scr.ap()[tt * 128:(tt + 1) * 128, :],
                out_dst=lambda tt: x1scr.ap()[tt * 128:(tt + 1) * 128, :],
                dbg_pair=(dbg_t["z1"], dbg_t["x1"]) if DBG_BUILD[0] else None,
                x1T_build=True,
                markpfx="p3")

    def phase4a(kind, eng):
        mm_part(
            kind, eng,
            rhs_src=lambda dw, fc: agw_out.ap()[dw * DM + fc * 128:dw * DM + (fc + 1) * 128, :],
            lhs_sb=x1T,
            bias_sb=fbb_sb,
            res_src=lambda dw, tt: x1scr.ap()[tt * 128:(tt + 1) * 128,
                                              dw * 512:(dw + 1) * 512],
            zdst=lambda dw, tt: z2scr.ap()[tt * 128:(tt + 1) * 128,
                                           dw * 512:(dw + 1) * 512],
            lhs_wait=None,
            agw_wait=True,
            markpfx="p4")

    def phase4b(kind, eng):
        ln_part(kind, eng,
                zsrc=lambda tt: z2scr.ap()[tt * 128:(tt + 1) * 128, :],
                out_dst=lambda tt: out[tt * 128:(tt + 1) * 128, :],
                dbg_pair=None,
                x1T_build=False,
                markpfx="p4")

    run_phase(phase3b)
    run_phase(phase4a)
    run_phase(phase4b)
    stX.close()
    stW.close()
    st.close()
    return nc


_host_consts = {}


def _consts():
    if _host_consts:
        return _host_consts
    half = 64
    inv = 1.0 / (10000.0 ** (np.arange(half, dtype=np.float64) / half))
    ang = np.arange(S, dtype=np.float64)[None, :] * inv[:, None]
    _host_consts["cos"] = np.ascontiguousarray(np.cos(ang).astype(np.float32))
    _host_consts["sin"] = np.ascontiguousarray(np.sin(ang).astype(np.float32))
    cm = np.empty((128, 4 * 512), np.float32)
    for o in range(4):
        keep = (np.arange(512)[None, :] - np.arange(128)[:, None]) >= o * 128
        cm[:, o * 512:(o + 1) * 512] = np.where(keep, np.float32(0.0), np.float32(NEG))
    _host_consts["cm"] = cm
    return _host_consts


_nc_cache = {}


def _get_nc(dbg):
    key = bool(dbg)
    if key not in _nc_cache:
        DBG_BUILD[0] = key
        _nc_cache[key] = build_nc(dbg=key)
    return _nc_cache[key]


def make_in_maps(inputs):
    cst = _consts()
    x = np.ascontiguousarray(np.asarray(inputs["L0_attn_hidden_states"]).reshape(T, DM))
    Wq = np.asarray(inputs["L0_attn_Wq"])
    bq = np.asarray(inputs["L0_attn_bq"])
    Wk = np.asarray(inputs["L0_attn_Wk"])
    bk = np.asarray(inputs["L0_attn_bk"])
    Wv = np.asarray(inputs["L0_attn_Wv"])
    bv = np.asarray(inputs["L0_attn_bv"])
    Wo = np.ascontiguousarray(np.asarray(inputs["L0_attn_Wo"]))
    bo = np.asarray(inputs["L0_attn_bo"])
    pkf = np.asarray(inputs["L0_ffn_weight_packed"]).reshape(DM, 2048)
    scf = np.asarray(inputs["L0_ffn_weight_scales"]).reshape(DM, 32)
    fbv = np.asarray(inputs["L0_ffn_bias"])
    in_maps = []
    for c in range(8):
        in_maps.append({
            "xck": np.ascontiguousarray(x[c * TC:(c + 1) * TC]),
            "wqd": np.ascontiguousarray(Wq[:, c * QH:(c + 1) * QH]),
            "bqv": np.ascontiguousarray(bq[c * QH:(c + 1) * QH].reshape(R, 128).T),
            "wkd": np.ascontiguousarray(Wk[:, c * D:(c + 1) * D]),
            "bkv": np.ascontiguousarray(bk[c * D:(c + 1) * D].reshape(1, 128).T),
            "wvd": np.ascontiguousarray(Wv[:, c * D:(c + 1) * D]),
            "bvv": np.ascontiguousarray(bv[c * D:(c + 1) * D].reshape(1, 128).T),
            "wod": Wo,
            "bod": np.ascontiguousarray(bo.reshape(1, DM)),
            "pkd": np.ascontiguousarray(pkf[c * 512:(c + 1) * 512]),
            "scd": np.ascontiguousarray(scf[c * 512:(c + 1) * 512]),
            "fbd": np.ascontiguousarray(fbv.reshape(1, DM)),
            "cosd": cst["cos"],
            "sind": cst["sin"],
            "cmd": cst["cm"],
        })
    return in_maps


def kernel(**inputs):
    nc = _get_nc(False)
    in_maps = make_in_maps(inputs)
    res = run_bass_kernel_spmd(nc, in_maps, core_ids=list(range(8)))
    o = np.concatenate([res.results[c]["out"] for c in range(8)], axis=0)
    return np.ascontiguousarray(o.reshape(B, S, DM))


# version 07
# speedup vs baseline: 1.1779x; 1.0017x over previous
import numpy as np
import ml_dtypes
from contextlib import ExitStack

import concourse.bass as bass
import concourse.mybir as mybir
from concourse.bass_utils import run_bass_kernel_spmd
from concourse.aot_env import allow_ici_all_to_all

F32 = mybir.dt.float32
BF16 = mybir.dt.bfloat16
I32 = mybir.dt.int32
AF = mybir.ActivationFunctionType
OP = mybir.AluOpType

B, S, DM = 2, 4096, 4096
T = B * S
TC = T // 8
H, G, D, R = 32, 8, 128, 4
QH = R * D
SCALE = 1.0 / float(np.sqrt(np.float64(128.0)))
EXPB = -20.0
NEG = -3.0e38
EPS = 1e-5
NT = TC // 128
NDM = DM // 128
NG = S // 512

DBG_BUILD = [False]


class Px:
    def __init__(self):
        self.eng = None

    def __getattr__(self, name):
        if self.eng is not None:
            return getattr(self.eng, name)
        return lambda *a, **kw: None


class Sem:
    def __init__(self, h):
        self.h = h
        self.n = 0

    def inc(self, inst, k=1):
        if inst is not None:
            inst.then_inc(self.h, k)
        self.n += k
        return self.n

    def wait(self, px, val=None):
        if px.eng is not None:
            px.eng.wait_ge(self.h, self.n if val is None else val)


def build_nc(dbg=False):
    nc = bass.Bass("TRN2", target_bir_lowering=False, debug=False)

    xck = nc.dram_tensor("xck", [TC, DM], F32, kind="ExternalInput").ap()
    wqd = nc.dram_tensor("wqd", [DM, QH], F32, kind="ExternalInput").ap()
    bqv = nc.dram_tensor("bqv", [128, R], F32, kind="ExternalInput").ap()
    wkd = nc.dram_tensor("wkd", [DM, D], F32, kind="ExternalInput").ap()
    bkv = nc.dram_tensor("bkv", [128, 1], F32, kind="ExternalInput").ap()
    wvd = nc.dram_tensor("wvd", [DM, D], F32, kind="ExternalInput").ap()
    bvv = nc.dram_tensor("bvv", [128, 1], F32, kind="ExternalInput").ap()
    wod = nc.dram_tensor("wod", [DM, DM], F32, kind="ExternalInput").ap()
    bod = nc.dram_tensor("bod", [1, DM], F32, kind="ExternalInput").ap()
    pkd = nc.dram_tensor("pkd", [512, 2048], I32, kind="ExternalInput").ap()
    scd = nc.dram_tensor("scd", [512, 32], F32, kind="ExternalInput").ap()
    fbd = nc.dram_tensor("fbd", [1, DM], F32, kind="ExternalInput").ap()
    cosd = nc.dram_tensor("cosd", [64, S], F32, kind="ExternalInput").ap()
    sind = nc.dram_tensor("sind", [64, S], F32, kind="ExternalInput").ap()
    cmd = nc.dram_tensor("cmd", [128, 4 * 512], F32, kind="ExternalInput").ap()
    out = nc.dram_tensor("out", [TC, DM], F32, kind="ExternalOutput").ap()

    agx_in = nc.dram_tensor("agx_in", [DM, TC], BF16)
    agx_out = nc.dram_tensor("agx_out", [8 * DM, TC], BF16)
    agw_in = nc.dram_tensor("agw_in", [DM, 512], BF16)
    agw_out = nc.dram_tensor("agw_out", [8 * DM, 512], BF16)
    a2a_in = nc.dram_tensor("a2a_in", [8, QH, TC], BF16)
    a2a_out = nc.dram_tensor("a2a_out", [8, QH, TC], BF16)
    z1scr = nc.dram_tensor("z1scr", [TC, DM], F32)
    x1scr = nc.dram_tensor("x1scr", [TC, DM], F32)
    z2scr = nc.dram_tensor("z2scr", [TC, DM], F32)

    dbg_t = {}
    if dbg:
        dbg_t["qT"] = nc.dram_tensor("d_qT", [128, R * S], F32, kind="ExternalOutput").ap()
        dbg_t["kT"] = nc.dram_tensor("d_kT", [128, T], F32, kind="ExternalOutput").ap()
        dbg_t["vb"] = nc.dram_tensor("d_vb", [128, T], F32, kind="ExternalOutput").ap()
        dbg_t["ctx"] = nc.dram_tensor("d_ctx", [8 * QH, TC], F32, kind="ExternalOutput").ap()
        dbg_t["z1"] = nc.dram_tensor("d_z1", [TC, DM], F32, kind="ExternalOutput").ap()
        dbg_t["x1"] = nc.dram_tensor("d_x1", [TC, DM], F32, kind="ExternalOutput").ap()
        dbg_t["wt"] = nc.dram_tensor("d_wt", [DM, 512], F32, kind="ExternalOutput").ap()
        dbg_t["win"] = nc.dram_tensor("d_win", [DM, 512], F32, kind="ExternalOutput").ap()
        dbg_t["wtile0"] = nc.dram_tensor("d_wtile0", [128, DM], F32, kind="ExternalOutput").ap()
        dbg_t["wtile1"] = nc.dram_tensor("d_wtile1", [128, DM], F32, kind="ExternalOutput").ap()

    st = ExitStack()

    def sball(stk, nm, sh, dt):
        return stk.enter_context(nc.sbuf_tensor(nm, sh, dt))

    sY = Sem(st.enter_context(nc.semaphore("sY")))
    sG = Sem(st.enter_context(nc.semaphore("sG")))
    sD = Sem(st.enter_context(nc.semaphore("sD")))
    sT = Sem(st.enter_context(nc.semaphore("sT")))
    sV = Sem(st.enter_context(nc.semaphore("sV")))
    sS = Sem(st.enter_context(nc.semaphore("sS")))
    sC = Sem(st.enter_context(nc.semaphore("sC")))
    sO = Sem(st.enter_context(nc.semaphore("sO")))
    s_init = Sem(st.enter_context(nc.semaphore("s_init")))
    s_bias = Sem(st.enter_context(nc.semaphore("s_bias")))
    s_cm = Sem(st.enter_context(nc.semaphore("s_cm")))
    s_tab = Sem(st.enter_context(nc.semaphore("s_tab")))
    s_ct = Sem(st.enter_context(nc.semaphore("s_ct")))
    s_st = [Sem(st.enter_context(nc.semaphore("s_st%d" % i))) for i in range(4)]
    s_xf = [Sem(st.enter_context(nc.semaphore("s_xf%d" % i))) for i in range(2)]
    s_xg = [Sem(st.enter_context(nc.semaphore("s_xg%d" % i))) for i in range(4)]
    s_qd = [Sem(st.enter_context(nc.semaphore("s_qd%d" % i))) for i in range(2)]
    s_o2 = [Sem(st.enter_context(nc.semaphore("s_o2%d" % i))) for i in range(2)]
    s_wo = [Sem(st.enter_context(nc.semaphore("s_wo%d" % i))) for i in range(2)]
    s_xr = [Sem(st.enter_context(nc.semaphore("s_xr%d" % i))) for i in range(2)]
    s_zs = [Sem(st.enter_context(nc.semaphore("s_zs%d" % i))) for i in range(2)]
    all_sems = ([sY, sG, sD, sT, sV, sS, sC, sO, s_init, s_bias, s_cm, s_tab, s_ct]
                + s_st + s_xf + s_xg + s_qd + s_o2 + s_wo + s_xr + s_zs)

    pall = st.enter_context(nc.psum_tensor("pall", [128, 4096], F32))
    pbank = [pall.ap()[:, i * 512:(i + 1) * 512] for i in range(8)]
    pallb = pall.ap().bitcast(BF16)

    identf = sball(st, "identf", [128, 128], F32)
    io_a = sball(st, "io_a", [128, 128], I32)
    io_b = sball(st, "io_b", [128, 128], I32)
    onecol = sball(st, "onecol", [128, 1], BF16)
    onerow = sball(st, "onerow", [1, 128], F32)
    cexpb = sball(st, "cexpb", [128, 1], F32)
    cepsb = sball(st, "cepsb", [128, 1], F32)
    bq_sb = sball(st, "bq_sb", [128, R], F32)
    bk_sb = sball(st, "bk_sb", [128, 1], F32)
    bv_sb = sball(st, "bv_sb", [128, 1], F32)

    def run_phase(fn):
        snap = [s_.n for s_ in all_sems]

        def mk(kind):
            def f(eng):
                for s_, v in zip(all_sems, snap):
                    s_.n = v
                fn(kind, eng)
            return f

        blk = nc.Block()
        blk.__enter__()
        blk.gpsimd(mk('g'))
        blk.vector(mk('v'))
        blk.scalar(mk('s'))
        blk.tensor(mk('t'))
        blk.sync(mk('y'))
        blk.__exit__(None, None, None)

    marks = {}

    ph0 = ExitStack()
    xf32 = [sball(ph0, "xf%d" % i, [128, DM], F32) for i in range(2)]
    xb8 = sball(ph0, "xb8", [128, NT * DM], BF16)
    identb = sball(ph0, "identb", [128, 128], BF16)
    xTrow = [sball(ph0, "xTr%d" % i, [128, TC], BF16) for i in range(2)]
    pkt = sball(ph0, "pkt", [128, 2048], I32)
    sct = sball(ph0, "sct", [128, 32], F32)
    d_n = sball(ph0, "d_n", [128, 1024], I32)
    d_s = sball(ph0, "d_s", [128, 1024], I32)
    d_i = sball(ph0, "d_i", [128, 1024], I32)
    d_e = sball(ph0, "d_e", [128, 1024], I32)
    d_m = sball(ph0, "d_m", [128, 1024], I32)
    d_a = sball(ph0, "d_a", [128, 1024], F32)
    d_u = sball(ph0, "d_u", [128, 1024], F32)
    d_q = sball(ph0, "d_q", [128, 1024], F32)
    d_t = sball(ph0, "d_t", [128, 1024], F32)
    d_g = sball(ph0, "d_g", [128, 1024], F32)
    wtile = sball(ph0, "wtile", [128, DM], F32)
    wall = sball(ph0, "wall", [128, 4 * DM], BF16)
    wTrow = [sball(ph0, "wTw%d" % i, [128, 512], BF16) for i in range(2)]

    def phase0(kind, eng):
        PXs = {k: Px() for k in "gvsty"}
        PXs[kind].eng = eng
        GE, V, SE, TE, Y = PXs['g'], PXs['v'], PXs['s'], PXs['t'], PXs['y']

        GE.memset(onecol[:], 1.0)
        GE.memset(onerow[:], 1.0)
        GE.memset(cexpb[:], EXPB)
        GE.memset(cepsb[:], EPS)
        GE.iota(io_a[:], [[1, 128]], base=0, channel_multiplier=0,
                allow_small_or_imprecise_dtypes=True)
        s_init.inc(GE.iota(io_b[:], [[0, 128]], base=0, channel_multiplier=1,
                           allow_small_or_imprecise_dtypes=True))
        s_bias.inc(GE.dma_start(out=bq_sb[:], in_=bqv), 16)
        s_bias.inc(GE.dma_start(out=bk_sb[:], in_=bkv), 16)
        s_bias.inc(GE.dma_start(out=bv_sb[:], in_=bvv), 16)
        s_init.wait(V)
        idm = sV.inc(V.tensor_tensor(identf[:], io_a[:], io_b[:], OP.is_equal))
        idb = sV.inc(V.tensor_copy(identb[:], identf[:]))
        sV.wait(TE, idb)

        cast_m = {}
        for t in range(NT):
            ldx = s_xf[t % 2].inc(Y.dma_start(out=xf32[t % 2][:],
                                              in_=xck[t * 128:(t + 1) * 128, :]), 16)
            s_xf[t % 2].wait(V, ldx)
            cast_m[t] = sV.inc(V.tensor_copy(xb8[:, t * DM:(t + 1) * DM], xf32[t % 2][:]))
            if t + 2 < NT:
                sV.wait(Y, cast_m[t])
        pbb = [pallb[:, (4 + i) * 1024:(4 + i) * 1024 + 128] for i in range(4)]
        trm = {}
        evm = {}
        stm = {}
        for db in range(NDM):
            for t in range(NT):
                i = db * NT + t
                sV.wait(TE, cast_m[t])
                if i >= 4:
                    sV.wait(TE, evm[i - 4])
                trm[i] = sT.inc(TE.transpose(
                    pbb[i % 4], xb8[:, t * DM + db * 128:t * DM + (db + 1) * 128],
                    identb[:]))
                sT.wait(V, trm[i])
                if db >= 2 and t == 0:
                    s_st[db % 2].wait(V, stm[db - 2])
                evm[i] = sV.inc(V.tensor_copy(
                    xTrow[db % 2][:, t * 128:(t + 1) * 128], pbb[i % 4]))
            sV.wait(Y, evm[db * NT + NT - 1])
            stm[db] = s_st[db % 2].inc(Y.dma_start(
                out=agx_in.ap()[db * 128:(db + 1) * 128, :], in_=xTrow[db % 2][:]), 16)
        for s_ in s_st[0:2]:
            s_.wait(GE)
        marks["agx"] = sC.inc(GE.collective_compute(
            "AllGather", OP.bypass, replica_groups=[list(range(8))],
            ins=[agx_in.ap().opt()], outs=[agx_out.ap().opt()]))

        vdeq = {}
        for ot in range(4):
            if ot >= 1:
                sV.wait(GE, vdeq[ot - 1])
            lp = sG.inc(GE.dma_start(out=pkt[:], in_=pkd[ot * 128:(ot + 1) * 128, :]), 16)
            lp = sG.inc(GE.dma_start(out=sct[:], in_=scd[ot * 128:(ot + 1) * 128, :]), 16)
            sG.wait(V, lp)
            for half in range(2):
                co = half * 1024
                pslice = pkt[:, co:co + 1024]
                for hi in range(2):
                    if hi == 0:
                        V.tensor_single_scalar(d_n[:], pslice, 4, OP.logical_shift_right)
                    else:
                        V.tensor_single_scalar(d_n[:], pslice, 15, OP.bitwise_and)
                    V.tensor_single_scalar(d_s[:], d_n[:], 3, OP.logical_shift_right)
                    V.tensor_single_scalar(d_i[:], d_n[:], 7, OP.bitwise_and)
                    V.tensor_single_scalar(d_e[:], d_i[:], 1, OP.logical_shift_right)
                    V.tensor_single_scalar(d_m[:], d_i[:], 1, OP.bitwise_and)
                    V.tensor_scalar(d_a[:], d_e[:], 2.0, 2.0, OP.mult, OP.min)
                    V.tensor_scalar(d_u[:], d_e[:], 0.5, 0.5, OP.mult, OP.max)
                    V.tensor_scalar_add(d_u[:], d_u[:], -0.25)
                    V.tensor_mul(d_q[:], d_u[:], d_u[:])
                    V.scalar_tensor_tensor(d_t[:], d_m[:], 1.0, d_a[:], OP.mult, OP.add)
                    V.scalar_tensor_tensor(d_g[:], d_q[:], 0.4375, d_t[:], OP.add, OP.mult)
                    V.tensor_scalar(d_t[:], d_s[:], -2.0, 1.0, OP.mult, OP.add)
                    V.tensor_mul(d_g[:], d_g[:], d_t[:])
                    ssrc = bass.AP(sct, half * 16, [[32, 128], [1, 16], [0, 64]])
                    dsts = bass.AP(wtile, 2 * co + hi, [[DM, 128], [2, 1024]])
                    V.tensor_tensor(dsts, d_g[:], ssrc, OP.mult)
            vdeq[ot] = sV.inc(V.tensor_copy(wall[:, ot * DM:(ot + 1) * DM], wtile[:]))
        wtrm = {}
        wevm = {}
        wstm = {}
        for ib in range(NDM):
            for ot in range(4):
                i = ib * 4 + ot
                sV.wait(TE, vdeq[ot])
                if i >= 4:
                    sV.wait(TE, wevm[i - 4])
                wtrm[i] = sT.inc(TE.transpose(
                    pbb[i % 4], wall[:, ot * DM + ib * 128:ot * DM + (ib + 1) * 128],
                    identb[:]))
                sT.wait(V, wtrm[i])
                if ib >= 2 and ot == 0:
                    s_st[2 + ib % 2].wait(V, wstm[ib - 2])
                wevm[i] = sV.inc(V.tensor_copy(
                    wTrow[ib % 2][:, ot * 128:(ot + 1) * 128], pbb[i % 4]))
            sV.wait(Y, wevm[ib * 4 + 3])
            wstm[ib] = s_st[2 + ib % 2].inc(Y.dma_start(
                out=agw_in.ap()[ib * 128:(ib + 1) * 128, :], in_=wTrow[ib % 2][:]), 16)
        for s_ in s_st[2:4]:
            s_.wait(GE)
        sC.wait(GE, marks["agx"])
        marks["agw"] = sC.inc(GE.collective_compute(
            "AllGather", OP.bypass, replica_groups=[list(range(8))],
            ins=[agw_in.ap().opt()], outs=[agw_out.ap().opt()]))

    run_phase(phase0)
    ph0.close()

    stA = ExitStack()
    wq_sb = sball(stA, "wq_sb", [128, NDM * QH], BF16)
    wk_sb = sball(stA, "wk_sb", [128, NDM * D], BF16)
    wv_sb = sball(stA, "wv_sb", [128, NDM * D], BF16)
    csl = [sball(stA, "csl0", [64, 512], F32)]
    snl = [sball(stA, "snl0", [64, 512], F32)]
    cm_sb = sball(stA, "cm_sb", [128, 4 * 512], BF16)
    qT_sb = sball(stA, "qT_sb", [128, R * S], BF16)
    kT_sb = sball(stA, "kT_sb", [128, T], BF16)
    vb_sb = sball(stA, "vb_sb", [128, T], BF16)
    xg = [sball(stA, "xg%d" % i, [128, 8, 512], BF16) for i in range(4)]
    qa = [sball(stA, "qa%d" % i, [128, 512], F32) for i in range(2)]
    qb = [sball(stA, "qb%d" % i, [64, 512], F32) for i in range(2)]
    rp = [sball(stA, "rp%d" % i, [64, 512], F32) for i in range(2)]
    o2s = [sball(stA, "o2s%d" % i, [64, 512], BF16) for i in range(2)]
    vst = sball(stA, "vst", [128, 512], F32)
    prT = sball(stA, "prT", [128, 32 * 512], BF16)
    dstg = [sball(stA, "dstg0", [128, 512], F32)]
    ctxe = [sball(stA, "ctxe0", [128, 512], BF16)]

    def phase1(b):
        def fn(kind, eng):
            PXs = {k: Px() for k in "gvsty"}
            PXs[kind].eng = eng
            GE, V, SE, TE, Y = PXs['g'], PXs['v'], PXs['s'], PXs['t'], PXs['y']

            if b == 0:
                for dmc in range(NDM):
                    sG.inc(GE.dma_start(out=wq_sb[:, dmc * QH:(dmc + 1) * QH],
                                        in_=wqd[dmc * 128:(dmc + 1) * 128, :]), 16)
                    sG.inc(GE.dma_start(out=wk_sb[:, dmc * D:(dmc + 1) * D],
                                        in_=wkd[dmc * 128:(dmc + 1) * 128, :]), 16)
                    sG.inc(GE.dma_start(out=wv_sb[:, dmc * D:(dmc + 1) * D],
                                        in_=wvd[dmc * 128:(dmc + 1) * 128, :]), 16)
                s_cm.inc(GE.dma_start(out=cm_sb[:], in_=cmd), 16)
                marks["wqkv"] = sG.n
                marks["cmld"] = s_cm.n

            mmhalf = {}
            ldm = {}
            evfb = {}
            ropev = {}
            roped = {}
            qadma = {}
            vtr = {}
            vev = {}
            for tg in range(8):
                gtok = b * S + tg * 512
                rank = gtok // TC
                coff = gtok % TC
                cl = tg * 512
                if tg >= 1:
                    sV.wait(Y, ropev[(tg - 1) * 5 + 4])
                s_tab.inc(Y.dma_start(out=csl[0][:], in_=cosd[:, cl:cl + 512]), 16)
                tabm = s_tab.inc(Y.dma_start(out=snl[0][:], in_=sind[:, cl:cl + 512]), 16)
                for q4 in range(4):
                    if tg > 0:
                        sT.wait(Y, mmhalf[(tg - 1, q4)])
                    lm = 0
                    sC.wait(Y, marks["agx"])
                    for j in range(8):
                        dmc = q4 * 8 + j
                        lm = s_xg[q4].inc(Y.dma_start(
                            out=xg[q4][:, j, :],
                            in_=agx_out.ap()[rank * DM + dmc * 128:rank * DM + dmc * 128 + 128,
                                             coff:coff + 512]), 16)
                    ldm[(tg, q4)] = lm
                sG.wait(TE, marks["wqkv"])
                for q4 in range(4):
                    s_xg[q4].wait(TE, ldm[(tg, q4)])
                    for j in range(8):
                        dmc = q4 * 8 + j
                        first = (dmc == 0)
                        last = (dmc == NDM - 1)
                        if first and tg > 0:
                            for bkk in range(6):
                                sV.wait(TE, evfb[(tg - 1, bkk)])
                        TE.matmul(pbank[0][:, 0:512],
                                  wq_sb[:, dmc * QH + 0 * 128:dmc * QH + 1 * 128],
                                  xg[q4][:, j, :], start=first, stop=last)
                        TE.matmul(pbank[1][:, 0:512],
                                  wq_sb[:, dmc * QH + 1 * 128:dmc * QH + 2 * 128],
                                  xg[q4][:, j, :], start=first, stop=last)
                        TE.matmul(pbank[2][:, 0:512],
                                  wq_sb[:, dmc * QH + 2 * 128:dmc * QH + 3 * 128],
                                  xg[q4][:, j, :], start=first, stop=last)
                        TE.matmul(pbank[3][:, 0:512],
                                  wq_sb[:, dmc * QH + 3 * 128:dmc * QH + 4 * 128],
                                  xg[q4][:, j, :], start=first, stop=last)
                        TE.matmul(pbank[4][:, 0:512],
                                  wk_sb[:, dmc * D:(dmc + 1) * D],
                                  xg[q4][:, j, :], start=first, stop=last)
                        inst = TE.matmul(pbank[5][:, 0:512],
                                         wv_sb[:, dmc * D:(dmc + 1) * D],
                                         xg[q4][:, j, :], start=first, stop=last)
                        if j == 7:
                            mmhalf[(tg, q4)] = sT.inc(inst)
                mmend = mmhalf[(tg, 3)]
                sT.wait(V, mmend)
                if b == 0 and tg == 0:
                    s_bias.wait(V, 48)
                s_tab.wait(V, tabm)
                cs = csl[0][:]
                sn = snl[0][:]
                for fb in range(5):
                    qi = fb % 2
                    tix = tg * 5 + fb
                    bias = bq_sb[:, fb:fb + 1] if fb < 4 else bk_sb[:, 0:1]
                    prev_qi = tix - {0: 1, 1: 3, 2: 2, 3: 2, 4: 2}[fb]
                    if prev_qi >= 0:
                        s_qd[qi].wait(V, qadma.get(prev_qi, 0) or 0)
                    evfb[(tg, fb)] = sV.inc(V.tensor_scalar(
                        qa[qi][:], pbank[fb][:, 0:512], bias, None, OP.add))
                    sV.wait(SE, evfb[(tg, fb)])
                    if prev_qi >= 0:
                        sV.wait(SE, ropev.get(prev_qi, 0) or 0)
                    qadma[tix] = s_qd[qi].inc(SE.dma_start(out=qb[qi][:], in_=qa[qi][64:128, :]), 16)
                    s_qd[qi].wait(V, qadma[tix])
                    dst_lo = (qT_sb[0:64, fb * S + cl:fb * S + cl + 512] if fb < 4
                              else kT_sb[0:64, gtok:gtok + 512])
                    V.tensor_tensor(rp[0][:], qa[qi][0:64, :], cs, OP.mult)
                    V.tensor_tensor(rp[1][:], qb[qi][:], sn, OP.mult)
                    V.tensor_sub(dst_lo, rp[0][:], rp[1][:])
                    V.tensor_tensor(rp[0][:], qa[qi][0:64, :], sn, OP.mult)
                    V.tensor_tensor(rp[1][:], qb[qi][:], cs, OP.mult)
                    if tix >= 2:
                        s_o2[tix % 2].wait(V, roped.get(tix - 2, 0) or 0)
                    ropev[tix] = sV.inc(V.tensor_add(o2s[tix % 2][:], rp[0][:], rp[1][:]))
                    sV.wait(SE, ropev[tix])
                    if fb < 4:
                        dst_hi = qT_sb[64:128, fb * S + cl:fb * S + cl + 512]
                    else:
                        dst_hi = kT_sb[64:128, gtok:gtok + 512]
                    roped[tix] = s_o2[tix % 2].inc(SE.dma_start(out=dst_hi, in_=o2s[tix % 2][:]), 16)
                if tg > 0:
                    sT.wait(V, vtr[(tg - 1) * 4 + 3])
                evfb[(tg, 5)] = sV.inc(V.tensor_scalar(
                    vst[:], pbank[5][:, 0:512], bv_sb[:, 0:1], None, OP.add))
                for j in range(4):
                    i = tg * 4 + j
                    sV.wait(TE, evfb[(tg, 5)])
                    if i >= 2:
                        sV.wait(TE, vev[i - 2])
                    vtr[i] = sT.inc(TE.transpose(pbank[6 + i % 2][:, 0:128],
                                                 vst[:, j * 128:(j + 1) * 128], identf[:]))
                    sT.wait(V, vtr[i])
                    vev[i] = sV.inc(V.tensor_copy(
                        vb_sb[:, gtok + j * 128:gtok + (j + 1) * 128],
                        pbank[6 + i % 2][:, 0:128]))
            marks["ropeD%d" % b] = (s_o2[0].n, s_o2[1].n, s_qd[0].n, s_qd[1].n)
        return fn

    def phase2(b):
        def fn(kind, eng):
            PXs = {k: Px() for k in "gvsty"}
            PXs[kind].eng = eng
            GE, V, SE, TE, Y = PXs['g'], PXs['v'], PXs['s'], PXs['t'], PXs['y']
            boff = b * S
            m0, m1, m2, m3 = marks["ropeD%d" % b]
            s_o2[0].wait(TE, m0)
            s_o2[1].wait(TE, m1)
            s_qd[0].wait(TE, m2)
            s_qd[1].wait(TE, m3)
            s_cm.wait(V, marks["cmld"])

            pafree = {0: None, 1: None}
            prread = {}
            dnfree = {0: None}
            bc6free = {0: None}
            cxfree = {0: None}
            ctfree = {0: None}
            dgfree = {0: None}
            dgbank = {6: None, 7: None}

            for grp in range(NG):
                nkb = 4 * (grp + 1)
                nfull = nkb - 4
                gq = boff + grp * 512
                dest = gq // TC
                qoff = gq % TC
                units = ([("pair", p) for p in range(nfull // 2)]
                         + [("diag", d) for d in range(4)])
                for h in range(4):
                    expm = {}
                    prev_kbs = None

                    def dnpv(kbs):
                        nonlocal_marker = None
                        for kd in kbs:
                            sS.wait(TE, expm[kd])
                            if kd == 0:
                                if dnfree[0] is not None:
                                    sV.wait(TE, dnfree[0])
                                if cxfree[0] is not None:
                                    sV.wait(TE, cxfree[0])
                            TE.matmul(pbank[5][0:1, 0:512], onecol[:],
                                      prT[:, kd * 512:(kd + 1) * 512],
                                      start=(kd == 0), stop=(kd == nkb - 1))
                            pv_in = TE.matmul(pbank[4][:, 0:512],
                                              vb_sb[:, boff + kd * 128:boff + (kd + 1) * 128],
                                              prT[:, kd * 512:(kd + 1) * 512],
                                              start=(kd == 0), stop=(kd == nkb - 1))
                            prread[kd] = sT.inc(pv_in)

                    for u, (kind_u, idx_u) in enumerate(units):
                        if kind_u == "pair":
                            kb0 = 2 * idx_u
                            sl = idx_u % 2
                            if pafree[sl] is not None:
                                sS.wait(TE, pafree[sl])
                            qrhs = qT_sb[:, h * S + grp * 512:h * S + (grp + 1) * 512]
                            TE.matmul(pall.ap()[:, sl * 1024:sl * 1024 + 512],
                                      kT_sb[:, boff + kb0 * 128:boff + (kb0 + 1) * 128],
                                      qrhs, start=True, stop=True)
                            mmm = sT.inc(TE.matmul(
                                pall.ap()[:, sl * 1024 + 512:sl * 1024 + 1024],
                                kT_sb[:, boff + (kb0 + 1) * 128:boff + (kb0 + 2) * 128],
                                qrhs, start=True, stop=True))
                            if prev_kbs is not None:
                                dnpv(prev_kbs)
                            sT.wait(SE, mmm)
                            if prread.get(kb0 + 1) is not None:
                                sT.wait(SE, prread[kb0 + 1])
                            if prread.get(kb0) is not None:
                                sT.wait(SE, prread[kb0])
                            ex = sS.inc(SE.activation(
                                prT[:, kb0 * 512:(kb0 + 2) * 512],
                                pall.ap()[:, sl * 1024:(sl + 1) * 1024],
                                AF.Exp, bias=cexpb[:], scale=SCALE))
                            expm[kb0] = expm[kb0 + 1] = ex
                            pafree[sl] = ex
                            prev_kbs = (kb0, kb0 + 1)
                        else:
                            d = idx_u
                            kb = nfull + d
                            oi = d
                            bnk = 6 + d % 2
                            if bnk == 6 and bc6free[0] is not None:
                                sV.wait(TE, bc6free[0])
                            if dgbank[bnk] is not None:
                                sV.wait(TE, dgbank[bnk])
                            mmm = sT.inc(TE.matmul(
                                pbank[bnk][:, 0:512],
                                kT_sb[:, boff + kb * 128:boff + (kb + 1) * 128],
                                qT_sb[:, h * S + grp * 512:h * S + (grp + 1) * 512],
                                start=True, stop=True))
                            if prev_kbs is not None:
                                dnpv(prev_kbs)
                            sT.wait(V, mmm)
                            if dgfree[0] is not None:
                                sS.wait(V, dgfree[0])
                            stmk = sV.inc(V.scalar_tensor_tensor(
                                dstg[0][:], pbank[bnk][:, 0:512], SCALE,
                                cm_sb[:, oi * 512:(oi + 1) * 512], OP.mult, OP.add))
                            dgbank[bnk] = stmk
                            sV.wait(SE, stmk)
                            if prread.get(kb) is not None:
                                sT.wait(SE, prread[kb])
                            ex = sS.inc(SE.activation(
                                prT[:, kb * 512:(kb + 1) * 512], dstg[0][:],
                                AF.Exp, bias=cexpb[:], scale=1.0))
                            expm[kb] = ex
                            dgfree[0] = ex
                            prev_kbs = (kb,)
                    dnpv(prev_kbs)
                    em = sT.n
                    sT.wait(V, em)
                    V.tensor_copy(rp[0][0:1, :], pbank[5][0:1, 0:512])
                    dn2 = sV.inc(V.reciprocal(rp[1][0:1, :], rp[0][0:1, :]))
                    dnfree[0] = dn2
                    sV.wait(TE, dn2)
                    dbm = sT.inc(TE.matmul(pbank[6][:, 0:512], onerow[:],
                                           rp[1][0:1, :], start=True, stop=True))
                    ci = 0
                    sT.wait(V, dbm)
                    bcm = sV.inc(V.tensor_copy(vst[:, 0:512], pbank[6][:, 0:512]))
                    bc6free[0] = bcm
                    if ctfree[ci] is not None:
                        sY.wait(V, ctfree[ci])
                    fm = sV.inc(V.tensor_tensor(ctxe[ci][:], pbank[4][:, 0:512],
                                                vst[:, 0:512], OP.mult))
                    cxfree[0] = fm
                    sV.wait(Y, fm)
                    ctfree[ci] = sY.inc(Y.dma_start(
                        out=a2a_in.ap()[dest, h * 128:(h + 1) * 128, qoff:qoff + 512],
                        in_=ctxe[ci][:]), 16)
        return fn

    run_phase(phase1(0))
    run_phase(phase2(0))
    run_phase(phase1(1))
    run_phase(phase2(1))

    def phase_a2a(kind, eng):
        PXs = {k: Px() for k in "gvsty"}
        PXs[kind].eng = eng
        GE = PXs['g']
        sY.wait(GE)
        with allow_ici_all_to_all():
            marks["a2a"] = sC.inc(GE.collective_compute(
                "AllToAll", OP.bypass, replica_groups=[list(range(8))],
                ins=[a2a_in.ap().opt()], outs=[a2a_out.ap().opt()]))
        if DBG_BUILD[0]:
            GE.wait_ge(sC.h, marks["a2a"])
            sO.inc(GE.dma_start(out=dbg_t["ctx"],
                                in_=a2a_out.ap().rearrange("a b c -> (a b) c")), 16)
            sO.inc(GE.dma_start(out=dbg_t["qT"], in_=qT_sb[:]), 16)
            sO.inc(GE.dma_start(out=dbg_t["kT"], in_=kT_sb[:]), 16)
            sO.inc(GE.dma_start(out=dbg_t["vb"], in_=vb_sb[:]), 16)
            sO.inc(GE.dma_start(out=dbg_t["wt"], in_=agw_out.ap()[0:DM, :]), 16)
            sO.inc(GE.dma_start(out=dbg_t["win"], in_=agw_in.ap()), 16)
            sO.wait(GE)

    run_phase(phase_a2a)
    stA.close()

    stW = ExitStack()
    woh = sball(stW, "woh", [128, NDM, 1024], BF16)
    bfb_sb = sball(stW, "bfb_sb", [128, DM], F32)
    bf_row = sball(stW, "bf_row", [1, 512], F32)
    zc = [sball(stW, "zc%d" % i, [128, 512], F32) for i in range(4)]
    zd = [sball(stW, "zd%d" % i, [128, 512], F32) for i in range(4)]
    xres = [sball(stW, "xres%d" % i, [128, 512], F32) for i in range(4)]
    sums = sball(stW, "sums", [128, NT * 8], F32)
    sqs = sball(stW, "sqs", [128, NT * 8], F32)
    stat = [sball(stW, "stat%d" % i, [128, 128], F32) for i in range(6)]
    r1c = sball(stW, "r1c", [128, 1], F32)
    r2c = sball(stW, "r2c", [128, 1], F32)
    zfull = sball(stW, "zfull", [128, DM], F32)
    x1t = sball(stW, "x1t", [128, DM], F32)

    stC = ExitStack()
    ctxT = sball(stC, "ctxT", [128, NDM * TC], BF16)

    def mm_part(kind, eng, *, rhs_src, lhs_sb, bias_sb, res_src, zdst,
                lhs_wait, agw_wait, markpfx):
        PXs = {k: Px() for k in "gvsty"}
        PXs[kind].eng = eng
        GE, V, SE, TE, Y = PXs['g'], PXs['v'], PXs['s'], PXs['t'], PXs['y']

        def build_bias(src_row):
            for w in range(8):
                bfl = s_bias.inc(GE.dma_start(out=bf_row[:],
                                              in_=src_row[0:1, w * 512:(w + 1) * 512]), 16)
                s_bias.wait(TE, bfl)
                m1 = sT.inc(TE.matmul(pbank[2][:, 0:512], onerow[:],
                                      bf_row[0:1, :], start=True, stop=True))
                sT.wait(V, m1)
                m2 = sV.inc(V.tensor_copy(bfb_sb[:, w * 512:(w + 1) * 512],
                                          pbank[2][:, 0:512]))
                sV.wait(TE, m2)
                sT.wait(GE, m1)

        if markpfx == "p3":
            sC.wait(Y, marks["a2a"])
            lm = 0
            a2f = a2a_out.ap().rearrange("a b c -> (a b) c")
            for fc in range(NDM):
                lm = s_ct.inc(Y.dma_start(out=ctxT[:, fc * TC:(fc + 1) * TC],
                                          in_=a2f[fc * 128:(fc + 1) * 128, :]), 16)
            marks["ctxT"] = lm
            build_bias(bod)
            marks["bcb"] = sV.n
        else:
            sO.wait(Y, marks["x1sO"])
            build_bias(fbd)

        wldm = {}
        mmend = {}
        e2m = {}
        e2m2 = {}
        sqm2 = {}
        stm2 = {}
        xldm = {}
        for dwp in range(4):
            for half in range(2):
                if dwp > 0:
                    sT.wait(GE, mmend[(dwp - 1, half)])
                if agw_wait:
                    sC.wait(GE, marks["agw"])
                lm = 0
                for fc in range(half * 16, half * 16 + 16):
                    for sub in range(2):
                        lm = s_wo[half].inc(GE.dma_start(
                            out=woh[:, fc, sub * 512:(sub + 1) * 512],
                            in_=rhs_src(dwp * 2 + sub, fc)), 16)
                wldm[(dwp, half)] = lm
            for tt in range(NT):
                bk = tt % 2
                i34 = dwp * NT + tt
                if lhs_wait == "ctxT":
                    s_ct.wait(TE, marks["ctxT"])
                if i34 >= 2:
                    sV.wait(TE, e2m[i34 - 2])
                for half in range(2):
                    s_wo[half].wait(TE, wldm[(dwp, half)])
                    inst = None
                    for fc in range(half * 16, half * 16 + 16):
                        TE.matmul(pbank[bk * 2][:, 0:512],
                                  lhs_sb[:, fc * TC + tt * 128:fc * TC + (tt + 1) * 128],
                                  woh[:, fc, 0:512],
                                  start=(fc == 0), stop=(fc == NDM - 1))
                        inst = TE.matmul(pbank[bk * 2 + 1][:, 0:512],
                                         lhs_sb[:, fc * TC + tt * 128:fc * TC + (tt + 1) * 128],
                                         woh[:, fc, 512:1024],
                                         start=(fc == 0), stop=(fc == NDM - 1))
                    mmend[(dwp, half)] = sT.inc(inst)
                m = mmend[(dwp, 1)]
                for sub in range(2):
                    dw = dwp * 2 + sub
                    i2 = i34 * 2 + sub
                    bx = bk * 2 + sub
                    if i2 >= 4:
                        sV.wait(Y, e2m2[i2 - 4])
                    xldm[i2] = s_xr[bk].inc(Y.dma_start(out=xres[bx][:],
                                                        in_=res_src(dw, tt)), 16)
                    if sub == 0:
                        sT.wait(V, m)
                        if markpfx == "p3":
                            sV.wait(V, marks["bcb"])
                    s_xr[bk].wait(V, xldm[i2])
                    if i2 >= 4:
                        sS.wait(V, sqm2[i2 - 4])
                        s_zs[bk].wait(V, stm2[i2 - 4])
                    V.tensor_tensor(zc[bx][:], pbank[bk * 2 + sub][:, 0:512],
                                    bias_sb[:, dw * 512:(dw + 1) * 512], OP.add)
                    ee = sV.inc(V.scalar_tensor_tensor(
                        zd[bx][:], zc[bx][:], 1.0, xres[bx][:], OP.mult, OP.add,
                        accum_out=sums[:, tt * 8 + dw:tt * 8 + dw + 1]))
                    e2m2[i2] = ee
                    if sub == 1:
                        e2m[i34] = ee
                    sV.wait(SE, ee)
                    sqm2[i2] = sS.inc(SE.activation(
                        zc[bx][:], zd[bx][:], AF.Square,
                        accum_out=sqs[:, tt * 8 + dw:tt * 8 + dw + 1]))
                    sS.wait(Y, sqm2[i2])
                    stm2[i2] = s_zs[bk].inc(Y.dma_start(out=zdst(dw, tt), in_=zd[bx][:]), 16)
        marks[markpfx + "sS"] = sS.n
        marks[markpfx + "sO"] = (s_zs[0].n, s_zs[1].n)

    def ln_part(kind, eng, *, zsrc, out_dst, dbg_pair, x1T_build, markpfx):
        PXs = {k: Px() for k in "gvsty"}
        PXs[kind].eng = eng
        GE, V, SE, TE, Y = PXs['g'], PXs['v'], PXs['s'], PXs['t'], PXs['y']
        xm = {}
        x1trm = {}
        x1evm = {}
        x1st = {}
        tr_i = 0
        z0, z1_ = marks[markpfx + "sO"]
        s_zs[0].wait(Y, z0)
        s_zs[1].wait(Y, z1_)
        for tt in range(NT):
            s_zs[0].wait(V, z0)
            s_zs[1].wait(V, z1_)
            sS.wait(V, marks[markpfx + "sS"])
            V.tensor_reduce(r1c[:], sums[:, tt * 8:tt * 8 + 8], mybir.AxisListType.X, OP.add)
            V.tensor_reduce(r2c[:], sqs[:, tt * 8:tt * 8 + 8], mybir.AxisListType.X, OP.add)
            V.tensor_copy(stat[0][:], bass.AP(r1c, 0, [[1, 128], [0, 128]]))
            V.tensor_copy(stat[1][:], bass.AP(r2c, 0, [[1, 128], [0, 128]]))
            V.tensor_scalar(stat[2][:], stat[0][:], 1.0 / DM, None, OP.mult)
            V.tensor_mul(stat[3][:], stat[2][:], stat[2][:])
            vm = sV.inc(V.scalar_tensor_tensor(stat[4][:], stat[1][:], 1.0 / DM,
                                               stat[3][:], OP.mult, OP.subtract))
            sV.wait(SE, vm)
            SE.activation(stat[5][:], stat[4][:], AF.Ln, bias=cepsb[:])
            sm = sS.inc(SE.activation(stat[5][:], stat[5][:], AF.Exp, scale=-0.5))
            sS.wait(V, sm)
            sV.inc(V.tensor_mul(stat[3][:], stat[2][:], stat[5][:]))
            if tt >= 1:
                sV.wait(Y, xm[tt - 1])
            zl = sY.inc(Y.dma_start(out=zfull[:], in_=zsrc(tt)), 16)
            sY.wait(V, zl)
            if tt >= 1:
                sO.wait(V, x1st[tt - 1])
                if x1T_build:
                    sT.wait(V, x1trm[(tt - 1) * NDM + NDM - 1])
            xm[tt] = sV.inc(V.tensor_scalar(x1t[:], zfull[:],
                                            stat[5][:, 0:1], stat[3][:, 0:1],
                                            OP.mult, OP.subtract))
            sV.wait(Y, xm[tt])
            x1st[tt] = sO.inc(Y.dma_start(out=out_dst(tt), in_=x1t[:]), 16)
            if dbg_pair is not None and DBG_BUILD[0]:
                sO.inc(Y.dma_start(out=dbg_pair[0][tt * 128:(tt + 1) * 128, :],
                                   in_=zfull[:]), 16)
                x1st[tt] = sO.inc(Y.dma_start(out=dbg_pair[1][tt * 128:(tt + 1) * 128, :],
                                              in_=x1t[:]), 16)
            if x1T_build:
                for ic in range(NDM):
                    i = tr_i
                    tr_i += 1
                    sV.wait(TE, xm[tt])
                    if i >= 2:
                        sV.wait(TE, x1evm[i - 2])
                    x1trm[i] = sT.inc(TE.transpose(
                        pbank[6 + i % 2][:, 0:128],
                        x1t[:, ic * 128:(ic + 1) * 128], identf[:]))
                    sT.wait(V, x1trm[i])
                    x1evm[i] = sV.inc(V.tensor_copy(
                        x1T[:, ic * TC + tt * 128:ic * TC + (tt + 1) * 128],
                        pbank[6 + i % 2][:, 0:128]))
        if x1T_build:
            marks["x1Tv"] = sV.n
        marks["x1sO" if x1T_build else "outsO"] = sO.n
        sO.wait(Y)

    def phase3a(kind, eng):
        mm_part(
            kind, eng,
            rhs_src=lambda dw, fc: wod[fc * 128:(fc + 1) * 128, dw * 512:(dw + 1) * 512],
            lhs_sb=ctxT,
            bias_sb=bfb_sb,
            res_src=lambda dw, tt: xck[tt * 128:(tt + 1) * 128, dw * 512:(dw + 1) * 512],
            zdst=lambda dw, tt: z1scr.ap()[tt * 128:(tt + 1) * 128,
                                           dw * 512:(dw + 1) * 512],
            lhs_wait="ctxT",
            agw_wait=False,
            markpfx="p3")

    run_phase(phase3a)
    stC.close()
    stX = ExitStack()
    x1T = sball(stX, "x1T", [128, NDM * TC], BF16)

    def phase3b(kind, eng):
        ln_part(kind, eng,
                zsrc=lambda tt: z1scr.ap()[tt * 128:(tt + 1) * 128, :],
                out_dst=lambda tt: x1scr.ap()[tt * 128:(tt + 1) * 128, :],
                dbg_pair=(dbg_t["z1"], dbg_t["x1"]) if DBG_BUILD[0] else None,
                x1T_build=True,
                markpfx="p3")

    def phase4a(kind, eng):
        mm_part(
            kind, eng,
            rhs_src=lambda dw, fc: agw_out.ap()[dw * DM + fc * 128:dw * DM + (fc + 1) * 128, :],
            lhs_sb=x1T,
            bias_sb=bfb_sb,
            res_src=lambda dw, tt: x1scr.ap()[tt * 128:(tt + 1) * 128,
                                              dw * 512:(dw + 1) * 512],
            zdst=lambda dw, tt: z2scr.ap()[tt * 128:(tt + 1) * 128,
                                           dw * 512:(dw + 1) * 512],
            lhs_wait=None,
            agw_wait=True,
            markpfx="p4")

    def phase4b(kind, eng):
        ln_part(kind, eng,
                zsrc=lambda tt: z2scr.ap()[tt * 128:(tt + 1) * 128, :],
                out_dst=lambda tt: out[tt * 128:(tt + 1) * 128, :],
                dbg_pair=None,
                x1T_build=False,
                markpfx="p4")

    run_phase(phase3b)
    run_phase(phase4a)
    run_phase(phase4b)
    stX.close()
    stW.close()
    st.close()
    return nc


_host_consts = {}


def _consts():
    if _host_consts:
        return _host_consts
    half = 64
    inv = 1.0 / (10000.0 ** (np.arange(half, dtype=np.float64) / half))
    ang = np.arange(S, dtype=np.float64)[None, :] * inv[:, None]
    _host_consts["cos"] = np.ascontiguousarray(np.cos(ang).astype(np.float32))
    _host_consts["sin"] = np.ascontiguousarray(np.sin(ang).astype(np.float32))
    cm = np.empty((128, 4 * 512), np.float32)
    for o in range(4):
        keep = (np.arange(512)[None, :] - np.arange(128)[:, None]) >= o * 128
        cm[:, o * 512:(o + 1) * 512] = np.where(keep, np.float32(0.0), np.float32(NEG))
    _host_consts["cm"] = cm
    return _host_consts


_nc_cache = {}


def _get_nc(dbg):
    key = bool(dbg)
    if key not in _nc_cache:
        DBG_BUILD[0] = key
        _nc_cache[key] = build_nc(dbg=key)
    return _nc_cache[key]


def make_in_maps(inputs):
    cst = _consts()
    x = np.ascontiguousarray(np.asarray(inputs["L0_attn_hidden_states"]).reshape(T, DM))
    Wq = np.asarray(inputs["L0_attn_Wq"])
    bq = np.asarray(inputs["L0_attn_bq"])
    Wk = np.asarray(inputs["L0_attn_Wk"])
    bk = np.asarray(inputs["L0_attn_bk"])
    Wv = np.asarray(inputs["L0_attn_Wv"])
    bv = np.asarray(inputs["L0_attn_bv"])
    Wo = np.ascontiguousarray(np.asarray(inputs["L0_attn_Wo"]))
    bo = np.asarray(inputs["L0_attn_bo"])
    pkf = np.asarray(inputs["L0_ffn_weight_packed"]).reshape(DM, 2048)
    scf = np.asarray(inputs["L0_ffn_weight_scales"]).reshape(DM, 32)
    fbv = np.asarray(inputs["L0_ffn_bias"])
    in_maps = []
    for c in range(8):
        in_maps.append({
            "xck": np.ascontiguousarray(x[c * TC:(c + 1) * TC]),
            "wqd": np.ascontiguousarray(Wq[:, c * QH:(c + 1) * QH]),
            "bqv": np.ascontiguousarray(bq[c * QH:(c + 1) * QH].reshape(R, 128).T),
            "wkd": np.ascontiguousarray(Wk[:, c * D:(c + 1) * D]),
            "bkv": np.ascontiguousarray(bk[c * D:(c + 1) * D].reshape(1, 128).T),
            "wvd": np.ascontiguousarray(Wv[:, c * D:(c + 1) * D]),
            "bvv": np.ascontiguousarray(bv[c * D:(c + 1) * D].reshape(1, 128).T),
            "wod": Wo,
            "bod": np.ascontiguousarray(bo.reshape(1, DM)),
            "pkd": np.ascontiguousarray(pkf[c * 512:(c + 1) * 512]),
            "scd": np.ascontiguousarray(scf[c * 512:(c + 1) * 512]),
            "fbd": np.ascontiguousarray(fbv.reshape(1, DM)),
            "cosd": cst["cos"],
            "sind": cst["sin"],
            "cmd": cst["cm"],
        })
    return in_maps


def kernel(**inputs):
    nc = _get_nc(False)
    in_maps = make_in_maps(inputs)
    res = run_bass_kernel_spmd(nc, in_maps, core_ids=list(range(8)))
    o = np.concatenate([res.results[c]["out"] for c in range(8)], axis=0)
    return np.ascontiguousarray(o.reshape(B, S, DM))
